# version 01 (final)
# baseline (speedup 1.0000x reference)
import numpy as np
import ml_dtypes
import concourse.bass as bass
import concourse.mybir as mybir
from concourse.bass_utils import run_bass_kernel_spmd

F32 = mybir.dt.float32
F16 = mybir.dt.float16
BF16 = mybir.dt.bfloat16
I32 = mybir.dt.int32
U32 = mybir.dt.uint32
AF = mybir.ActivationFunctionType
ALU = mybir.AluOpType
AX = mybir.AxisListType

B, T, C = 2, 4096, 2048
NH, LAT = 16, 128
HID0 = 5461
D_IN_PROJ = 8352
D_INNER = 4096
NHEADS = 32
D_STATE = 64
CONV_DIM = 4224
D_CONV = 4
CHUNK = 256
HID1 = 8192
TQ = 1024
NT = TQ // 128
SM_SCALE = float(1.0 / np.sqrt(LAT))
LN_EPS = 1e-5
RMS_EPS_ATTN = 1e-5
RMS_EPS_FFN = 1e-6

ENGS = ("te", "ve", "sc", "gp", "sy")


def ceil_div(a, b):
    return (a + b - 1) // b


class Prog:

    def __init__(self, nc):
        self.nc = nc
        self.boot = None
        self.sem = {}
        self.ctr = {}
        self.waited = {}
        self.ctx = ""
        self.records = []
        self._cur = self.records
        self._switch_stack = []

    NRING = 6
    NRING_B = 4

    def init_sems(self, stack):
        for tag in ("", "b0", "b1", "b2", "b3"):
            for e in ENGS:
                key = tag + e
                h = stack.enter_context(self.nc.semaphore("sem_" + (key or "g")))
                self.sem[key] = h
                self.ctr[key] = 0
        self.ring_sems = {}
        self.ring_ctr = {}
        self.ring_idx = {}
        for e in ("sy", "gp", "sc"):
            for i in range(self.NRING):
                k = "R" + e + str(i)
                self.ring_sems[k] = stack.enter_context(self.nc.semaphore("sem_" + k))
                self.sem[k] = self.ring_sems[k]
            for i in range(self.NRING_B):
                k = "Q" + e + str(i)
                self.ring_sems[k] = stack.enter_context(self.nc.semaphore("sem_" + k))
                self.sem[k] = self.ring_sems[k]

    def key(self, e):
        return (self.ctx + e) if self.ctx else e

    def emit(self, e, deps, fn, n_inc=1):
        ck = self.key(e)
        if self.boot is not None:
            deps = list(deps) + [self.boot]
        waits = []
        for (sk, val) in deps:
            if sk == ck and val >= self.ctr[ck] + n_inc:
                raise RuntimeError("dep on future self token")
            w = self.waited.get((ck, sk), 0)
            if val > w:
                waits.append((sk, val))
                self.waited[(ck, sk)] = val
        self.ctr[ck] += n_inc
        self._cur.append(("i", e, ck, waits, fn, n_inc, self.ctr[ck]))
        return (ck, self.ctr[ck])

    def dma(self, e, deps, out, in_):
        if self.ctx:
            n = self.NRING_B
            pre = "Q" + e
            ctr_key = (self.ctx, e)
        else:
            n = self.NRING
            pre = "R" + e
            ctr_key = ("", e)
        if self.boot is not None:
            deps = list(deps) + [self.boot]
        idx = self.ring_idx.get(ctr_key, 0)
        self.ring_idx[ctr_key] = (idx + 1) % n
        sk = pre + str(idx)
        cnt = self.ring_ctr.get((self.ctx, sk), 0)
        deps = list(deps)
        if cnt > 0:
            deps.append((sk, 16 * cnt))
        self.ring_ctr[(self.ctx, sk)] = cnt + 1
        ck = self.key(e)
        waits = []
        for (skk, val) in deps:
            w = self.waited.get((ck, skk), 0)
            if val > w:
                waits.append((skk, val))
                self.waited[(ck, skk)] = val
        self._cur.append(("i", e, sk, waits,
                          lambda g, out=out, in_=in_: g.dma_start(out=out, in_=in_),
                          16, 16 * (cnt + 1)))
        return (sk, 16 * (cnt + 1))

    def tok(self, e):
        ck = self.key(e)
        return (ck, self.ctr[ck])

    def ring_join(self, e):
        pre = ("Q" if self.ctx else "R") + e
        n = self.NRING_B if self.ctx else self.NRING
        deps = []
        for i in range(n):
            sk = pre + str(i)
            cnt = self.ring_ctr.get((self.ctx, sk), 0)
            if cnt:
                deps.append((sk, 16 * cnt))
        return self.emit(e, deps, lambda g: g.nop())

    def begin_switch(self, nbodies, qsb_ap, q_dep):
        assert self.ctx == ""
        sw = ("s", nbodies, {b: [] for b in range(nbodies)}, qsb_ap, q_dep)
        self.records.append(sw)
        self._switch_stack.append(sw)
        self._saved_waited = dict(self.waited)

    def begin_body(self, body):
        sw = self._switch_stack[-1]
        self.ctx = "b%d" % body
        self._cur = sw[2][body]
        self.waited = dict(self._saved_waited)

    def end_body(self, used=ENGS):
        toks = []
        for e in used:
            waits = []
            for i in range(self.NRING_B):
                for qe in ("sy", "gp", "sc"):
                    sk = "Q" + qe + str(i)
                    cnt = self.ring_ctr.get((self.ctx, sk), 0)
                    if cnt > 0:
                        w = self.waited.get((self.ctx + e, sk), 0)
                        if 16 * cnt > w:
                            waits.append((sk, 16 * cnt))
                            self.waited[(self.ctx + e, sk)] = 16 * cnt
            for e2 in used:
                ck2 = self.ctx + e2
                cnt = self.ctr.get(ck2, 0)
                if cnt > 0:
                    w = self.waited.get((self.ctx + e, ck2), 0)
                    if cnt > w:
                        waits.append((ck2, cnt))
                        self.waited[(self.ctx + e, ck2)] = cnt
            self.ctr[e] += 1
            self._cur.append(("i", e, e, waits, lambda g: g.nop(), 1, self.ctr[e]))
            toks.append((e, self.ctr[e]))
        for e in used:
            self.ctr[e] -= 1
        self.ctx = ""
        self._cur = self.records if not self._switch_stack[:-1] else None
        return toks

    def end_switch(self, used=ENGS):
        self._switch_stack.pop()
        self._cur = self.records
        self.waited = self._saved_waited
        for e in used:
            self.ctr[e] += 1
        return [(e, self.ctr[e]) for e in used]

    def finalize(self, blk, engines_by_name):
        def replay_stream(e, g, recs):
            for r in recs:
                if r[0] == "i":
                    (_, eng, ck, waits, fn, n_inc, _v) = r
                    if eng != e:
                        continue
                    for (sk, val) in waits:
                        g.wait_ge(self.sem[sk], val)
                    inst = fn(g)
                    inst.then_inc(self.sem[ck], n_inc)
                else:
                    (_, nbodies, bodies, qsb_ap, q_dep) = r
                    (sk, val) = q_dep
                    g.wait_ge(self.sem[sk], val)
                    iv = g.value_load(qsb_ap)
                    for b in self.nc.Switch(engines=[g], index=[iv], n=nbodies):
                        replay_stream(e, g, bodies[b])

        def mk(e):
            def body(g):
                replay_stream(e, g, self.records)
            return body

        blk.tensor(mk("te"))
        blk.vector(mk("ve"))
        blk.scalar(mk("sc"))
        blk.gpsimd(mk("gp"))
        blk.sync(mk("sy"))



DEV_TAPS = []
DEV_STAGE = 99
DEV_SKIP = []


def build_nc(stage=None, taps=None):
    import contextlib
    stage = DEV_STAGE if stage is None else stage
    taps = list(DEV_TAPS if taps is None else taps)
    nc = bass.Bass("TRN2", target_bir_lowering=False)

    def din(name, shape, dtype=F32):
        return nc.dram_tensor(name, list(shape), dtype, kind="ExternalInput").ap()

    x_own = din("x_own", (TQ, C))
    x_full = din("x_full", (T, C))
    qcfg = din("qcfg", (1, 4), U32)
    W_latE = din("W_latE", (C, LAT + 1))
    blatE = din("blatE", (1, LAT + 1))
    W_d = din("W_d", (C, C))
    b_d = din("b_d", (128, 16))
    W_proj = din("W_proj", (C, C))
    b_proj = din("b_proj", (128, 16))
    w1T = din("w1T", (C, HID0))
    w3T = din("w3T", (C, HID0))
    w2T = din("w2T", (HID0, C))
    W_inT = din("W_inT", (C, D_IN_PROJ))
    Wconv = din("Wconv", (CONV_DIM, D_CONV))
    b_conv = din("b_conv", (128, CONV_DIM // 128))
    dt_bias = din("dt_bias", (NHEADS, 1))
    A_log = din("A_log", (NHEADS, 1))
    D_rep = din("D_rep", (128, NHEADS))
    norm_w = din("norm_w", (128, NHEADS))
    W_outT = din("W_outT", (D_INNER, C))
    f1T = din("f1T", (C, HID1))
    f2T = din("f2T", (HID1, C))
    gamma_row = din("gamma_row", (1, C))
    gamma_rep = din("gamma_rep", (128, C))
    ident = din("ident", (128, 128), BF16)
    ident32 = din("ident32", (128, 128))
    strip = din("strip", (128, 896), BF16)
    trineg = din("trineg", (128, 512), BF16)

    out_own = nc.dram_tensor("out_own", [TQ, C], F32, kind="ExternalOutput").ap()
    tap_aps = {}
    dbg32 = (nc.dram_tensor("tap_dbg32", [128, 1024], F32, kind="ExternalOutput").ap()
             if "dbg32" in taps else None)
    dbg16 = (nc.dram_tensor("tap_dbg16", [128, 1024], BF16, kind="ExternalOutput").ap()
             if "dbg32" in taps else None)
    dbgw = (nc.dram_tensor("tap_dbgw", [128, 4 * 6144], BF16, kind="ExternalOutput").ap()
            if "dbgw" in taps else None)

    winq_d = nc.dram_tensor("winq_d", [C, D_IN_PROJ], BF16, kind="Internal").ap()
    woq_d = nc.dram_tensor("woq_d", [D_INNER, C], BF16, kind="Internal").ap()
    xw_d = nc.dram_tensor("xw_d", [384, C], F32, kind="Internal").ap()
    rw_d = nc.dram_tensor("rw_d", [64, NHEADS * 128], F32, kind="Internal").ap()
    halo_d = nc.dram_tensor("halo_d", [128, 33 * 4], BF16, kind="Internal").ap()
    w1q_d = nc.dram_tensor("w1q_d", [C, HID0], BF16, kind="Internal").ap()
    w3q_d = nc.dram_tensor("w3q_d", [C, HID0], BF16, kind="Internal").ap()
    w2q_d = nc.dram_tensor("w2q_d", [HID0, C], BF16, kind="Internal").ap()
    f1q_d = nc.dram_tensor("f1q_d", [C, HID1], BF16, kind="Internal").ap()
    xq_d = nc.dram_tensor("xq_d", [C, TQ], BF16, kind="Internal").ap()
    f2q_d = nc.dram_tensor("f2q_d", [HID1, C], BF16, kind="Internal").ap()

    P = Prog(nc)
    with contextlib.ExitStack() as ctx:
        E = ctx.enter_context
        x_res = E(nc.sbuf_tensor("x_res", [128, NT * C], F32))
        xnT = E(nc.sbuf_tensor("xnT", [128, 16 * TQ], BF16))
        bigT = E(nc.sbuf_tensor("bigT", [128, 43 * TQ], BF16))
        stage16 = E(nc.sbuf_tensor("stage16", [128, 4096], BF16))
        stage32 = E(nc.sbuf_tensor("stage32", [128, 2048], F32))
        evac16 = E(nc.sbuf_tensor("evac16", [128, 1024], BF16))
        i32st = E(nc.sbuf_tensor("i32st", [128, 512], I32))
        sm = E(nc.sbuf_tensor("sm", [128, 256], F32))
        idsb = E(nc.sbuf_tensor("idsb", [128, 128], BF16))
        idsb32 = E(nc.sbuf_tensor("idsb32", [128, 128], F32))
        qsb = E(nc.sbuf_tensor("qsb", [1, 4], U32))
        onesc = E(nc.sbuf_tensor("onesc", [128, 2], BF16))
        ones1r = E(nc.sbuf_tensor("ones1r", [1, 128], BF16))
        onesc32 = E(nc.sbuf_tensor("onesc32", [128, 2], F32))
        ones1r32 = E(nc.sbuf_tensor("ones1r32", [1, 128], F32))
        l1w = E(nc.sbuf_tensor("l1w", [128, 240], F32)) if stage >= 5 else None

        def bigv(a, b):
            return bigT[:, a:b]
        psum = [E(nc.psum_tensor("pb%d" % i, [128, 512], F32)) for i in range(7)]
        psT = E(nc.psum_tensor("pbT", [128, 512], BF16))

        P.init_sems(ctx)
        blk = E(nc.Block())
        emit, dma, tok = P.emit, P.dma, P.tok

        def xres(i):
            return x_res[:, i * C:(i + 1) * C]

        def xnT_d(i):
            return xnT[:, i * TQ:(i + 1) * TQ]

        if taps:
            for _tb in (bigT, xnT, evac16, stage16, stage32, sm, i32st):
                emit("ve", [], lambda g, _tb=_tb: g.memset(_tb[:], 0.0))
            P.boot = tok("ve")
        t_id = dma("sy", [], idsb[:], ident[:])
        t_id32 = dma("sy", [], idsb32[:], ident32[:])
        t_q = dma("sy", [], qsb[:], qcfg[:])
        t_x = [dma("sy", [], xres(i), x_own[i * 128:(i + 1) * 128, :]) for i in range(NT)]
        t_ones = emit("ve", [], lambda g: g.memset(onesc[:], 1.0))
        t_ones = emit("ve", [], lambda g: g.memset(ones1r[:], 1.0))
        t_ones = emit("ve", [], lambda g: g.memset(onesc32[:], 1.0))
        t_ones = emit("ve", [], lambda g: g.memset(ones1r32[:], 1.0))

        def bcast_col(src11_tok, src11, dst_col, pbank, np_=128):
            tm = emit("te", [src11_tok, t_ones, tok("sc"), tok("ve")],
                      lambda g: g.matmul(pbank[0:np_, 0:1], ones1r32[0:1, 0:np_], src11, start=True, stop=True))
            tc = emit("sc", [tm], lambda g: g.copy(dst_col, pbank[0:np_, 0:1]))
            return tc

        def rsqrt_newton(dep, t_col, out_col, tmpa, tmpb):
            t1 = emit("sc", [dep], lambda g: g.activation(tmpa, t_col, AF.Sqrt))
            t2 = emit("ve", [t1], lambda g: g.reciprocal(out_col, tmpa))
            for _ in range(2):
                t2 = emit("ve", [t2], lambda g: g.tensor_tensor(tmpa, out_col, out_col, op=ALU.mult))
                t2 = emit("ve", [t2], lambda g: g.tensor_tensor(tmpa, tmpa, t_col, op=ALU.mult))
                t2 = emit("ve", [t2], lambda g: g.tensor_scalar(
                    tmpa, tmpa, -0.5, 1.5, op0=ALU.mult, op1=ALU.add))
                t2 = emit("ve", [t2], lambda g: g.tensor_tensor(out_col, out_col, tmpa, op=ALU.mult))
            return t2

        def ln_stats(tile_tok, src_tile, mean_col, rstd_col, eps, pbank):
            W = src_tile.shape[-1]
            nch = ceil_div(W, 512)
            t1 = None
            for cch in range(nch):
                wc = min(512, W - cch * 512)
                t1 = emit("ve", [tile_tok], lambda g, cch=cch, wc=wc: g.bn_stats(
                    sm[:, 64 + 6 * cch:70 + 6 * cch], src_tile[:, cch * 512:cch * 512 + wc]))
            t2 = emit("ve", [t1], lambda g: g.bn_aggr(sm[:, 88:90], sm[:, 64:64 + 6 * nch]))
            t3 = emit("ve", [t2], lambda g: g.tensor_copy(mean_col, sm[:, 88:89]))
            t4 = emit("ve", [t2, constdep(eps)], lambda g, eps=eps: g.tensor_tensor(
                sm[:, 90:91], sm[:, 89:90], constcol(eps), op=ALU.add))
            t5 = rsqrt_newton(t4, sm[:, 90:91], rstd_col, sm[:, 91:92], sm[:, 92:93])
            return t3, t5, t2

        def transpose128(in_tok, src128, dst128, extra_deps=()):
            tt = emit("te", [in_tok, t_id, *extra_deps],
                      lambda g: g.transpose(psT[:, 0:128], src128, idsb[:]))
            tc = emit("sc", [tt], lambda g: g.copy(dst128, psT[:, 0:128]))
            return tc

        tp_state = {"i": 0, "last": [None, None, None, None]}

        def transpose128r(in_tok, src128, dst128):
            w = src128.shape[-1]
            p_in = src128.shape[0]
            deps = [in_tok, t_id]
            if tp_state["last"][0] is not None:
                deps.append(tp_state["last"][0])
            tt = emit("te", deps, lambda g: g.transpose(
                psT[0:w, 0:p_in], src128, idsb[0:p_in, 0:p_in]))
            tc = emit("sc", [tt], lambda g: g.copy(dst128, psT[0:w, 0:p_in]))
            tp_state["last"][0] = tc
            return tc

        sm_alloc = {"n": 100}

        def sm_col():
            i = sm_alloc["n"]
            sm_alloc["n"] += 1
            assert sm_alloc["n"] <= 256
            return sm[:, i:i + 1], i

        _constcols = {}

        def constcol(val):
            if val not in _constcols:
                col, _ = sm_col()
                t = emit("ve", [], lambda g, col=col, val=val: g.memset(col, float(val)))
                _constcols[val] = (col, t)
            return _constcols[val][0]

        def constdep(val):
            constcol(val)
            return _constcols[val][1]

        acc32 = sm

        def quant_matrix(wT_ap, q_ap, R, Cc, guard_eps=None):
            ent_gp = P.ring_join("gp")
            ent = [P.ring_join("sy"), tok("ve"), tok("sc"), ent_gp]
            n_r = ceil_div(R, 128)
            n_c = ceil_div(Cc, 512)
            ntile = n_r * n_c
            s_col, _ = sm_col()
            invs_col, _ = sm_col()
            t_z = emit("ve", [], lambda g: g.memset(sm[:, 0:1], 0.0))
            t_zp = emit("ve", [t_z], lambda g: g.memset(sm[:, 1:2], 0.0))
            slot_rd = [None, None]
            for i in range(ntile):
                r, cc = divmod(i, n_c)
                pr = min(128, R - r * 128)
                w = min(512, Cc - cc * 512)
                k = i % 2
                buf = stage32[0:pr, k * 512:k * 512 + w]
                deps = [slot_rd[k]] if slot_rd[k] else list(ent)
                t_ld = dma("sy", deps, buf, wT_ap[r * 128:r * 128 + pr, cc * 512:cc * 512 + w])
                t_ab = emit("sc", [t_ld, t_zp], lambda g, buf=buf, pr=pr, w=w: g.activation(
                    stage32[0:pr, 1024:1024 + w], buf, AF.Abs,
                    accum_out=sm[0:pr, 1:2]))
                slot_rd[k] = t_ab
                t_acc = emit("ve", [t_ab], lambda g, pr=pr: g.tensor_tensor(
                    sm[0:pr, 0:1], sm[0:pr, 0:1], sm[0:pr, 1:2], op=ALU.add))
                t_zp = t_acc
            t_red = tok("ve")
            t_tr = emit("te", [t_red, t_id32, tok("sc"), tok("ve")], lambda g: g.transpose(
                psum[6][0:1, 0:128], sm[:, 0:1], idsb32[:]))
            t_cp = emit("sc", [t_tr], lambda g: g.copy(stage32[0:1, 1024:1152], psum[6][0:1, 0:128]))
            t_rs = emit("ve", [t_cp], lambda g: g.tensor_reduce(
                stage32[0:1, 1152:1153], stage32[0:1, 1024:1152], axis=AX.X, op=ALU.add))
            t_row = emit("ve", [t_rs, t_ones], lambda g: g.tensor_scalar(
                stage32[0:1, 1024:1152], ones1r32[0:1, 0:128], stage32[0:1, 1152:1153],
                None, op0=ALU.mult))
            t_tr2 = emit("te", [t_row, t_cp], lambda g: g.transpose(
                psum[6][0:128, 0:1], stage32[0:1, 1024:1152], idsb32[0:1, 0:1]))
            t_tot = emit("sc", [t_tr2], lambda g: g.copy(sm[:, 2:3], psum[6][0:128, 0:1]))
            t_s = emit("ve", [t_tot], lambda g: g.tensor_scalar(
                sm[:, 2:3], sm[:, 2:3], float(1.0 / (R * Cc)), None, op0=ALU.mult))
            if guard_eps is not None:
                t_s = emit("ve", [t_s], lambda g: g.tensor_scalar(
                    sm[:, 2:3], sm[:, 2:3], float(guard_eps), None, op0=ALU.max))
            hp_col, _ = sm_col()
            hn_col, _ = sm_col()
            t_bs = emit("ve", [t_s], lambda g: g.tensor_scalar(
                s_col, sm[:, 2:3], -1.0, None, op0=ALU.mult))
            t_bp = emit("ve", [t_s], lambda g: g.tensor_scalar(
                hp_col, sm[:, 2:3], 0.5, None, op0=ALU.mult))
            t_bn = emit("ve", [t_s], lambda g: g.tensor_scalar(
                hn_col, sm[:, 2:3], -0.5, None, op0=ALU.mult))
            t_bi = t_bn
            slot_rd = [None, None]
            u_rd = [None, None]
            q_rd = [ent_gp, ent_gp]
            last = None
            for i in range(ntile):
                r, cc = divmod(i, n_c)
                pr = min(128, R - r * 128)
                w = min(512, Cc - cc * 512)
                k = i % 2
                buf = stage32[0:pr, k * 512:k * 512 + w]
                ubuf = stage32[0:pr, 1024 + k * 512:1024 + k * 512 + w]
                qbuf = evac16[0:pr, k * 512:k * 512 + w]
                deps = [slot_rd[k], t_bp, t_bn] if slot_rd[k] else [t_bp, t_bn]
                t_ld = dma("sy", deps, buf, wT_ap[r * 128:r * 128 + pr, cc * 512:cc * 512 + w])
                du = [t_ld] + ([u_rd[k]] if u_rd[k] else [])
                t_a = emit("ve", du, lambda g, buf=buf, ubuf=ubuf, pr=pr: g.tensor_scalar(
                    ubuf, buf, hp_col[0:pr, 0:1], None, op0=ALU.is_ge))
                dq = [t_a] + ([q_rd[k]] if q_rd[k] else [])
                t_q = emit("ve", dq, lambda g, buf=buf, ubuf=ubuf, qbuf=qbuf, pr=pr:
                           g.scalar_tensor_tensor(
                               qbuf, buf, hn_col[0:pr, 0:1], ubuf,
                               op0=ALU.is_le, op1=ALU.subtract))
                slot_rd[k] = t_q
                u_rd[k] = t_q
                t_st = dma("gp", [t_q], q_ap[r * 128:r * 128 + pr, cc * 512:cc * 512 + w], qbuf)
                q_rd[k] = t_st
                last = t_st
            return last, s_col, hp_col


        if stage >= 2:
            tq_w1, s1_col, _ = quant_matrix(w1T, w1q_d, C, HID0)
            tq_w3, s3_col, _ = quant_matrix(w3T, w3q_d, C, HID0)
            tq_w2, s2_col, _ = quant_matrix(w2T, w2q_d, HID0, C)
        if stage >= 4:
            tq_f1, sf1_col, _ = quant_matrix(f1T, f1q_d, C, HID1, guard_eps=1e-5)
            tq_f2, sf2_col, _ = quant_matrix(f2T, f2q_d, HID1, C, guard_eps=1e-5)

        tq_win = None
        if stage >= 5:
            wc_ld = [None, None]
            wc_chunks = [(0, 2048), (2048, 2048), (4096, 2048), (6144, 2048), (8192, 160)]
            ent16 = [tok("ve")]
            jobs = [(W_inT, winq_d, r, c0, cw) for r in range(16)
                    for (c0, cw) in wc_chunks]
            jobs += [(W_outT, woq_d, r, 0, 2048) for r in range(32)]
            for ij, (srcT, dstT, r, c0, cw) in enumerate(jobs):
                k = ij % 2
                sl16 = stage16[:, k * 2048:k * 2048 + cw]
                dl = [wc_ld[k]] if wc_ld[k] else list(ent16)
                t_l = dma("gp", dl, sl16,
                          srcT[r * 128:(r + 1) * 128, c0:c0 + cw])
                t_s = dma("sy", [t_l], dstT[r * 128:(r + 1) * 128, c0:c0 + cw], sl16)
                wc_ld[k] = t_s
            tq_win = [wc_ld[0], wc_ld[1]]
        if stage >= 5:
            quant_done = P.emit("gp", [P.ring_join("sy"), P.ring_join("gp")],
                                lambda g: g.nop())
        else:
            quant_done = P.ring_join("gp") if stage >= 2 else None

        USE_WIN = stage >= 5 and "win" not in DEV_SKIP
        if stage >= 5:
            t_cw = dma("sy", [], l1w[:, 0:132], Wconv.rearrange("(a p) j -> p a j", p=128))
            t_cb5 = dma("sy", [], l1w[:, 132:165], b_conv[:])
            t_al = dma("sy", [], l1w[0:32, 165:166], A_log[:])
            t_db5 = dma("sy", [], l1w[0:32, 167:168], dt_bias[:])
            t_nw = dma("sy", [], l1w[:, 169:201], norm_w[:])
            t_dd5 = dma("sy", [], l1w[:, 201:233], D_rep[:])
            t_an = emit("sc", [t_al], lambda g: g.activation(
                l1w[0:32, 166:167], l1w[0:32, 165:166], AF.Exp))
            t_an = emit("ve", [t_an], lambda g: g.tensor_scalar(
                l1w[0:32, 166:167], l1w[0:32, 166:167], -1.0, None, op0=ALU.mult))
            l1c_done = [t_cw, t_cb5, t_db5, t_nw, t_dd5, t_an]
            t_xwp = [dma("gp", [], xw_d[j * 128:(j + 1) * 128, :],
                         x_full[j * 128:(j + 1) * 128, :]) for j in range(3)]

        b_d_sb = None
        b_proj_sb = None
        tr32a_guard = [None]
        if stage >= 3:
            bds = E(nc.sbuf_tensor("b_d_sb", [C // 128 * 128, 1], F32)) if False else None
            b_d_sb2 = E(nc.sbuf_tensor("b_d_t", [128, 16], F32))
            b_proj_sb2 = E(nc.sbuf_tensor("b_proj_t", [128, 16], F32))
            t_bd = dma("sy", [], b_d_sb2[:], b_d[:])
            t_bp2 = dma("sy", [], b_proj_sb2[:], b_proj[:])

            class _ColView:
                def __init__(self, t):
                    self.t = t

                def __getitem__(self, idx):
                    sl = idx[0]
                    j = sl.start // 128
                    return self.t[:, j:j + 1]

            b_d_sb = _ColView(b_d_sb2)
            b_proj_sb = _ColView(b_proj_sb2)

        def ln_transpose_tile(x_tok, src_tile, dst_tiles, war_dep, mcol, rcol, eps,
                              width=C, stg=None):
            stg = stg if stg is not None else stage16
            tm, tr, _ = ln_stats(x_tok, src_tile, mcol, rcol, eps, psum[6])
            deps = [tm, tr] + ([war_dep] if war_dep else [])
            t_ap = emit("ve", deps, lambda g: g.tensor_scalar(
                stg[:, 0:width], src_tile, mcol, rcol, op0=ALU.subtract, op1=ALU.mult))
            for d in range(width // 128):
                transpose128r(t_ap, stg[:, d * 128:(d + 1) * 128], dst_tiles[d])
            return t_ap, tok("te")

        war = quant_done if stage >= 5 else None
        for i in range(NT):
            dsts = [xnT_d(d)[:, i * 128:(i + 1) * 128] for d in range(16)]
            _, war = ln_transpose_tile(t_x[i], xres(i), dsts, war,
                                       sm[:, 4:5], sm[:, 5:6], LN_EPS)
        xnT_done = [tok("te"), tok("sc")]

        if stage >= 3:
            KB, KT0, YT0, ET0, QH0, STR0, XF0, WL0, WD0, BL0 = (
                0, 4224, 8320, 24704, 25728, 27776, 28672, 30720, 32784, 36880)

            def kbuf_t(i):
                return bigT[:, KB + i * 132:KB + i * 132 + 132]

            def yT_f(f):
                return bigT[:, YT0 + f * TQ:YT0 + (f + 1) * TQ]

            t_wl = dma("gp", [quant_done], bigT[:, WL0:WL0 + 16 * 129],
                       W_latE.rearrange("(k p) l -> p k l", p=128))
            t_bl = dma("gp", [], bigT[0:1, BL0:BL0 + 129], blatE[:])
            t_str = dma("gp", [], bigT[:, STR0:STR0 + 896], strip[:])

            k_guard = [None]

            def emit_k_tile(s_tile, xnT_tiles, deps):
                d0 = list(deps) + [t_wl, t_bl]
                if k_guard[0]:
                    d0.append(k_guard[0])
                for kk in range(16):
                    dd = d0 if kk == 0 else []
                    emit("te", dd, lambda g, kk=kk, s_tile=s_tile, xnT_tiles=xnT_tiles: g.matmul(
                        psum[4][:, 0:129], xnT_tiles[kk],
                        bigT[:, WL0 + kk * 129:WL0 + (kk + 1) * 129],
                        start=(kk == 0), stop=False))
                emit("te", [], lambda g, s_tile=s_tile: g.matmul(
                    psum[4][:, 0:129], ones1r[0:1, 0:128],
                    bigT[0:1, BL0:BL0 + 129], start=False, stop=True))
                t_mm = tok("te")
                t_ev = emit("sc", [t_mm], lambda g, s_tile=s_tile: g.copy(
                    kbuf_t(s_tile)[:, 0:129], psum[4][:, 0:129]))
                k_guard[0] = t_ev
                t_tr = transpose128r(t_ev, kbuf_t(s_tile)[:, 0:128],
                                     kT[:, s_tile * 128:(s_tile + 1) * 128])
                return t_ev, t_tr

            kT = bigT[:, KT0:KT0 + 4096]
            tp_pre_switch = tp_state["last"][0]
            P.begin_switch(4, qsb[0:1, 0:1], t_q)
            sw_toks = None
            for body in range(4):
                P.begin_body(body)
                tp_state["last"][0] = tp_pre_switch
                k_guard[0] = None
                nfor = 8 * body
                for_guard = None
                for st_i in range(nfor):
                    dldx = [quant_done] + ([for_guard] if for_guard else [])
                    t_ldx = dma("sy", dldx, stage32[:, 0:2048],
                                x_full[st_i * 128:(st_i + 1) * 128, :])
                    dsts = [bigT[:, XF0 + d * 128:XF0 + (d + 1) * 128] for d in range(16)]
                    t_ap, t_tr = ln_transpose_tile(t_ldx, stage32[:, 0:2048], dsts, None,
                                                   sm[:, 4:5], sm[:, 5:6], LN_EPS)
                    for_guard = t_ap
                    xt = [bigT[:, XF0 + d * 128:XF0 + (d + 1) * 128] for d in range(16)]
                    emit_k_tile(st_i, xt, [t_tr, tok("sc")])
                    if USE_WIN and st_i >= nfor - 3:
                        j = st_i - (nfor - 3)
                        for_guard = dma("gp", [t_ldx, t_ap, *t_xwp],
                                        xw_d[j * 128:(j + 1) * 128, :], stage32[:, 0:2048])
                for j in range(NT):
                    xt = [xnT_d(d)[:, j * 128:(j + 1) * 128] for d in range(16)]
                    emit_k_tile(nfor + j, xt, [*xnT_done])
                t_kdone = [tok("te"), tok("sc")]
                qh_guard = {0: None, 1: None}
                qev_guard = [None]
                o_guard = [None]
                for h in range(NH):
                    wd_sl = WD0 + (h % 2) * 2048
                    t_wd = dma("gp", [qh_guard[h % 2]] if qh_guard[h % 2] else [t_kdone[0]],
                               bigT[:, wd_sl:wd_sl + 2048],
                               W_d[:, h * 128:(h + 1) * 128].rearrange("(k p) f -> p k f", p=128))
                    for tc2 in range(2):
                        bank = psum[5]
                        for kk in range(16):
                            dd = ([t_wd, *xnT_done]
                                  + ([qev_guard[0]] if qev_guard[0] else [])
                                  + ([o_guard[0]] if o_guard[0] else [])) if kk == 0 else []
                            emit("te", dd, lambda g, kk=kk, tc2=tc2, wd_sl=wd_sl: g.matmul(
                                bank[:, 0:512],
                                bigT[:, wd_sl + kk * 128:wd_sl + (kk + 1) * 128],
                                xnT_d(kk)[:, tc2 * 512:(tc2 + 1) * 512],
                                start=(kk == 0), stop=(kk == 15)))
                        t_qmm = tok("te")
                        t_qe = emit("sc", [t_qmm, t_bd], lambda g, tc2=tc2, h=h: g.activation(
                            bigT[:, QH0 + tc2 * 512:QH0 + (tc2 + 1) * 512], bank[:, 0:512],
                            AF.Identity, bias=b_d_sb[h * 128:(h + 1) * 128, 0:1]))
                        qev_guard[0] = t_qe
                    t_qh = tok("sc")
                    qh_guard[h % 2] = t_qh
                    for tc in range(2):
                        n_s = nfor + 4 * tc + 4
                        diag0 = nfor + 4 * tc
                        et_guard = [None, None]
                        st_guard = [None, None]
                        o_first = True
                        for st_i in range(n_s):
                            sb = st_i % 2
                            dS = [t_qh, t_kdone[1]]
                            if st_guard[sb]:
                                dS.append(st_guard[sb])
                            emit("te", dS, lambda g, st_i=st_i, sb=sb, tc=tc: g.matmul(
                                psum[sb][:, 0:512],
                                kT[:, st_i * 128:(st_i + 1) * 128],
                                bigT[:, QH0 + tc * 512:QH0 + (tc + 1) * 512],
                                start=True, stop=True))
                            t_S = tok("te")
                            dE = [t_S]
                            if et_guard[sb]:
                                dE.append(et_guard[sb])
                            t_E = emit("sc", dE, lambda g, sb=sb: g.activation(
                                bigT[:, ET0 + sb * 512:ET0 + (sb + 1) * 512],
                                psum[sb][:, 0:512], AF.Exp, scale=SM_SCALE))
                            st_guard[sb] = t_E
                            if st_i >= diag0:
                                delta = (st_i - diag0) * 128
                                t_E = emit("ve", [t_E, t_str], lambda g, sb=sb, delta=delta: g.tensor_tensor(
                                    bigT[:, ET0 + sb * 512:ET0 + (sb + 1) * 512],
                                    bigT[:, ET0 + sb * 512:ET0 + (sb + 1) * 512],
                                    bigT[:, STR0 + 384 - delta:STR0 + 896 - delta],
                                    op=ALU.mult))
                            for sub in range(4):
                                ob = psum[2 + sub // 2]
                                oo = (sub % 2) * 129
                                dpv = [t_E]
                                if st_i == 0 and o_guard[0]:
                                    dpv.append(o_guard[0])
                                emit("te", dpv, lambda g, sb=sb, sub=sub, ob=ob, oo=oo,
                                     st_i=st_i, o_first=o_first, n_s=n_s: g.matmul(
                                    ob[:, oo:oo + 129],
                                    bigT[:, ET0 + sb * 512 + sub * 128:ET0 + sb * 512 + (sub + 1) * 128],
                                    kbuf_t(st_i)[:, 0:129],
                                    start=(st_i == 0 and sub % 2 == 0),
                                    stop=(st_i == n_s - 1 and sub % 2 == 1)))
                            et_guard[sb] = tok("te")
                            o_first = False
                        t_opv = tok("te")
                        t_prev = None
                        for sub in range(4):
                            ob = psum[2 + sub // 2]
                            oo = (sub % 2) * 129
                            dr = [t_opv] + ([t_prev] if t_prev else [])
                            t_r = emit("ve", dr, lambda g, ob=ob, oo=oo: g.reciprocal(
                                sm[:, 8:9], ob[:, oo + 128:oo + 129]))
                            dn = [t_r] + ([tp_state["last"][0]] if tp_state["last"][0] else [])
                            t_n = emit("ve", dn, lambda g, ob=ob, oo=oo: g.tensor_scalar(
                                evac16[:, 0:128], ob[:, oo:oo + 128], sm[:, 8:9], None,
                                op0=ALU.mult))
                            t_prev = transpose128r(t_n, evac16[:, 0:128],
                                          yT_f(h)[:, tc * 512 + sub * 128:tc * 512 + (sub + 1) * 128])
                        o_guard[0] = tok("ve")
                P.end_body()
            sw_toks = P.end_switch()
            tp_state["last"][0] = sw_toks[1]

            wp_bank_guard = {0: None, 1: None}
            wp_strip_guard = None
            for d128 in range(16):
                dstrip = [*sw_toks] + ([wp_strip_guard] if wp_strip_guard else [])
                t_wp = dma("gp", dstrip, stage16[:, 0:2048],
                           W_proj[:, d128 * 128:(d128 + 1) * 128].rearrange("(k p) f -> p k f", p=128))
                for th in range(2):
                    bank = psum[4 + th]
                    first = [t_wp, *sw_toks]
                    if wp_bank_guard[th]:
                        first.append(wp_bank_guard[th])
                    for f in range(16):
                        d = first if f == 0 else []
                        emit("te", d, lambda g, f=f, th=th, bank=bank: g.matmul(
                            bank[:, 0:512],
                            stage16[:, f * 128:(f + 1) * 128],
                            yT_f(f)[:, th * 512:(th + 1) * 512],
                            start=(f == 0), stop=(f == 15)))
                    t_mm = tok("te")
                    dev_ = [t_mm, t_bp2] + ([tr32a_guard[0]] if tr32a_guard[0] else [])
                    t_ev = emit("sc", dev_, lambda g, bank=bank, d128=d128: g.activation(
                        stage32[:, 1024:1536], bank[:, 0:512], AF.Identity,
                        bias=b_proj_sb[d128 * 128:(d128 + 1) * 128, 0:1]))
                    wp_bank_guard[th] = t_ev
                    for j in range(4):
                        t = th * 4 + j
                        dtt = [t_ev, t_id32] + ([tr32a_guard[0]] if tr32a_guard[0] else [])
                        tt = emit("te", dtt, lambda g, j=j: g.transpose(
                            psum[6][:, 128:256],
                            stage32[:, 1024 + j * 128:1024 + j * 128 + 128],
                            idsb32[:]))
                        ta = emit("ve", [tt], lambda g, t=t, d128=d128: g.tensor_tensor(
                            xres(t)[:, d128 * 128:(d128 + 1) * 128],
                            xres(t)[:, d128 * 128:(d128 + 1) * 128],
                            psum[6][:, 128:256], op=ALU.add))
                        tr32a_guard[0] = ta
                wp_strip_guard = tok("te")
            attn_done = [tok("ve")]

            if USE_WIN:
                WX0, YW0, XWT0, XW10, GW0, ZW0, CBW = 37504, 8320, 14464, 20608, 0, 0, 8580
                NF0 = ceil_div(HID0, 128)
                KSPLIT = 22
                wg = list(attn_done) + [tok("te"), tok("sc")]
                w_guard = None
                for j in range(3):
                    dl = wg + ([w_guard] if w_guard else [])
                    t_l = dma("sy", dl, stage32[:, 0:2048], xw_d[j * 128:(j + 1) * 128, :])
                    dsts = [bigT[:, WX0 + d * 384 + j * 128:WX0 + d * 384 + (j + 1) * 128]
                            for d in range(16)]
                    t_ap, _ = ln_transpose_tile(t_l, stage32[:, 0:2048], dsts,
                                                wg[0] if j == 0 else w_guard,
                                                sm[:, 4:5], sm[:, 5:6], LN_EPS)
                    t_cp = emit("ve", [t_l, tok("sc")], lambda g: g.tensor_copy(
                        stage16[:, 2048:4096], stage32[:, 0:2048]))
                    for d in range(16):
                        transpose128r(t_cp, stage16[:, 2048 + d * 128:2048 + (d + 1) * 128],
                                      bigT[:, XWT0 + d * 384 + j * 128:XWT0 + d * 384 + (j + 1) * 128])
                    w_guard = tok("ve")
                wx_done = [tok("te"), tok("sc"), tok("ve"), P.ring_join("sy")]
                if dbgw is not None:
                    wx_done.append(dma("gp", list(wx_done), dbgw[:, 0:6144],
                                       bigT[:, WX0:WX0 + 6144]))

                tp_pre2 = tp_state["last"][0]
                P.begin_switch(4, qsb[0:1, 0:1], t_q)
                for body in range(4):
                    P.begin_body(body)
                    tp_state["last"][0] = tp_pre2
                    if body > 0:
                        nfor = 8 * body
                        diag0 = nfor - 3
                        qwh_guard = [None]
                        o_guard = [None]
                        for h in range(NH):
                            dwd = [*wx_done] + ([qwh_guard[0]] if qwh_guard[0] else [])
                            t_wd = dma("gp", dwd, bigT[:, WD0:WD0 + 2048],
                                       W_d[:, h * 128:(h + 1) * 128].rearrange("(k p) f -> p k f", p=128))
                            for kk in range(16):
                                dd = ([t_wd, *wx_done]
                                      + ([o_guard[0]] if o_guard[0] else [])) if kk == 0 else []
                                emit("te", dd, lambda g, kk=kk: g.matmul(
                                    psum[5][:, 0:384], bigT[:, WD0 + kk * 128:WD0 + (kk + 1) * 128],
                                    bigT[:, WX0 + kk * 384:WX0 + (kk + 1) * 384],
                                    start=(kk == 0), stop=(kk == 15)))
                            t_qm = tok("te")
                            t_qe = emit("sc", [t_qm, t_bd], lambda g, h=h: g.activation(
                                bigT[:, QH0:QH0 + 384], psum[5][:, 0:384],
                                AF.Identity, bias=b_d_sb[h * 128:(h + 1) * 128, 0:1]))
                            qwh_guard[0] = t_qe
                            et_guard = [None, None]
                            st_guard = [None, None]
                            for st_i in range(nfor):
                                sb = st_i % 2
                                dS = [t_qe] + ([st_guard[sb]] if st_guard[sb] else [])
                                emit("te", dS, lambda g, st_i=st_i, sb=sb: g.matmul(
                                    psum[sb][:, 0:384], kT[:, st_i * 128:(st_i + 1) * 128],
                                    bigT[:, QH0:QH0 + 384], start=True, stop=True))
                                t_S = tok("te")
                                dE = [t_S] + ([et_guard[sb]] if et_guard[sb] else [])
                                t_E = emit("sc", dE, lambda g, sb=sb: g.activation(
                                    bigT[:, ET0 + sb * 512:ET0 + sb * 512 + 384],
                                    psum[sb][:, 0:384], AF.Exp, scale=SM_SCALE))
                                st_guard[sb] = t_E
                                if st_i >= diag0:
                                    delta = (st_i - diag0) * 128
                                    t_E = emit("ve", [t_E, t_str], lambda g, sb=sb, delta=delta:
                                               g.tensor_tensor(
                                        bigT[:, ET0 + sb * 512:ET0 + sb * 512 + 384],
                                        bigT[:, ET0 + sb * 512:ET0 + sb * 512 + 384],
                                        bigT[:, STR0 + 384 - delta:STR0 + 768 - delta],
                                        op=ALU.mult))
                                for sub in range(3):
                                    ob = psum[2 + sub // 2]
                                    oo = (sub % 2) * 129
                                    dpv = [t_E]
                                    if st_i == 0 and sub == 0 and o_guard[0]:
                                        dpv.append(o_guard[0])
                                    emit("te", dpv, lambda g, sb=sb, sub=sub, ob=ob, oo=oo,
                                         st_i=st_i, nfor=nfor: g.matmul(
                                        ob[:, oo:oo + 129],
                                        bigT[:, ET0 + sb * 512 + sub * 128:ET0 + sb * 512 + (sub + 1) * 128],
                                        kbuf_t(st_i)[:, 0:129],
                                        start=(st_i == 0 and (sub % 2 == 0)),
                                        stop=(st_i == nfor - 1 and (sub == 1 or sub == 2))))
                                et_guard[sb] = tok("te")
                            t_opv = tok("te")
                            t_prev = None
                            for sub in range(3):
                                ob = psum[2 + sub // 2]
                                oo = (sub % 2) * 129
                                dr = [t_opv] + ([t_prev] if t_prev else [])
                                t_r = emit("ve", dr, lambda g, ob=ob, oo=oo: g.reciprocal(
                                    sm[:, 8:9], ob[:, oo + 128:oo + 129]))
                                dn = [t_r] + ([tp_state["last"][0]] if tp_state["last"][0] else [])
                                t_n = emit("ve", dn, lambda g, ob=ob, oo=oo: g.tensor_scalar(
                                    evac16[:, 0:128], ob[:, oo:oo + 128], sm[:, 8:9], None,
                                    op0=ALU.mult))
                                t_prev = transpose128r(t_n, evac16[:, 0:128],
                                    bigT[:, YW0 + h * 384 + sub * 128:YW0 + h * 384 + (sub + 1) * 128])
                            o_guard[0] = tok("ve")
                    P.end_body()
                sw2 = P.end_switch()
                tp_state["last"][0] = sw2[1]
                if dbgw is not None:
                    sw2 = list(sw2) + [dma("gp", list(sw2), dbgw[:, 6144:12288],
                                          bigT[:, YW0:YW0 + 6144])]

                w3sg = None
                w3bg = None
                for d128 in range(16):
                    dwp = [*sw2] + ([w3sg] if w3sg else [])
                    t_wp = dma("gp", dwp, stage16[:, 0:2048],
                               W_proj[:, d128 * 128:(d128 + 1) * 128].rearrange("(k p) f -> p k f", p=128))
                    first = [t_wp, *sw2] + ([w3bg] if w3bg else [])
                    for f in range(16):
                        d = first if f == 0 else []
                        emit("te", d, lambda g, f=f: g.matmul(
                            psum[4][:, 0:384], stage16[:, f * 128:(f + 1) * 128],
                            bigT[:, YW0 + f * 384:YW0 + (f + 1) * 384],
                            start=(f == 0), stop=(f == 15)))
                    t_mm = tok("te")
                    dev_ = [t_mm, t_bp2] + ([w3bg] if w3bg else [])
                    t_ev = emit("sc", dev_, lambda g, d128=d128: g.activation(
                        stage32[:, 0:384], psum[4][:, 0:384], AF.Identity,
                        bias=b_proj_sb[d128 * 128:(d128 + 1) * 128, 0:1]))
                    w3bg = emit("ve", [t_ev, *wx_done], lambda g, d128=d128: g.tensor_tensor(
                        bigT[:, XW10 + d128 * 384:XW10 + (d128 + 1) * 384],
                        bigT[:, XWT0 + d128 * 384:XWT0 + (d128 + 1) * 384],
                        stage32[:, 0:384], op=ALU.add))
                    w3sg = tok("te")
                w3_done = [tok("te"), tok("ve"), tok("sc")]

                fch_w = [(c0, min(256, HID0 - c0)) for c0 in range(0, HID0, 256)]
                fwsg = None
                fwev = {}
                for (c0, w) in fch_w:
                    d1 = [tq_w1, *w3_done] + ([fwsg] if fwsg else [])
                    t_s1 = dma("sy", d1, stage16[:, 0:16 * w],
                               w1q_d.rearrange("(k p) h -> p k h", p=128)[:, :, c0:c0 + w])
                    for t in range(3):
                        bank = psum[t // 2]
                        off = (t % 2) * 256
                        fob = (t % 2 == 0)
                        d = [t_s1, *w3_done, tok("sc"), tok("ve")] if t == 0 else []
                        for kk in range(16):
                            dd = d if kk == 0 else []
                            emit("te", dd, lambda g, kk=kk, t=t, w=w, bank=bank, off=off,
                                 fob=fob: g.matmul(
                                bank[:, off:off + w],
                                bigT[:, WX0 + kk * 384 + t * 128:WX0 + kk * 384 + (t + 1) * 128],
                                stage16[:, kk * w:(kk + 1) * w],
                                start=(kk == 0 and fob), stop=(kk == 15 and (not fob or t == 2))))
                    t_h1 = tok("te")
                    for t in range(3):
                        bank = psum[t // 2]
                        off = (t % 2) * 256
                        emit("sc", [t_h1], lambda g, bank=bank, off=off, t=t, w=w: g.activation(
                            stage32[:, t * 256:t * 256 + w], bank[:, off:off + w], AF.Silu,
                            scale=s1_col))
                    t_silu = tok("sc")
                    t_s3 = dma("sy", [tq_w3, t_h1], stage16[:, 0:16 * w],
                               w3q_d.rearrange("(k p) h -> p k h", p=128)[:, :, c0:c0 + w])
                    for t in range(3):
                        bank = psum[t // 2]
                        off = (t % 2) * 256
                        fob = (t % 2 == 0)
                        d = [t_s3, t_silu] if t == 0 else []
                        for kk in range(16):
                            dd = d if kk == 0 else []
                            emit("te", dd, lambda g, kk=kk, t=t, w=w, bank=bank, off=off,
                                 fob=fob: g.matmul(
                                bank[:, off:off + w],
                                bigT[:, WX0 + kk * 384 + t * 128:WX0 + kk * 384 + (t + 1) * 128],
                                stage16[:, kk * w:(kk + 1) * w],
                                start=(kk == 0 and fob), stop=(kk == 15 and (not fob or t == 2))))
                    t_h3 = tok("te")
                    for t in range(3):
                        bank = psum[t // 2]
                        off = (t % 2) * 256
                        sl = t * 256
                        dg = [t_h3]
                        if fwev.get(sl):
                            dg.append(fwev[sl])
                        t_g = emit("ve", dg, lambda g, bank=bank, off=off, sl=sl, t=t, w=w:
                                   g.scalar_tensor_tensor(
                                       evac16[:, sl:sl + w], bank[:, off:off + w], s3_col,
                                       stage32[:, t * 256:t * 256 + w],
                                       op0=ALU.mult, op1=ALU.mult))
                        last_tt = None
                        for half in range(ceil_div(w, 128)):
                            hw = min(128, w - half * 128)
                            ft = c0 // 128 + half
                            last_tt = transpose128r(
                                t_g, evac16[:, sl + half * 128:sl + half * 128 + hw],
                                bigT[0:hw, GW0 + ft * 384 + t * 128:GW0 + ft * 384 + (t + 1) * 128])
                        fwev[sl] = last_tt
                    fwsg = tok("te")
                gw_done = [tok("te"), tok("sc")]
                w4sg = None
                w4bg = None
                for d128 in range(16):
                    for ks in range(2):
                        f_lo = 0 if ks == 0 else KSPLIT
                        f_hi = KSPLIT if ks == 0 else NF0
                        dstrip = [tq_w2, tok("te")] + ([w4sg] if w4sg else [])
                        full_end = min(f_hi, 42)
                        t_l1 = dma("sy", dstrip, stage16[:, 0:(full_end - f_lo) * 128],
                                   w2q_d[f_lo * 128:full_end * 128, :].rearrange(
                                       "(f p) c -> p f c", p=128)[:, :, d128 * 128:(d128 + 1) * 128])
                        t_l2 = None
                        if f_hi == NF0:
                            t_l2 = dma("sy", [], stage16[0:85, (42 - f_lo) * 128:(43 - f_lo) * 128],
                                       w2q_d[5376:5461, d128 * 128:(d128 + 1) * 128])
                        first = [t_l1, *gw_done] + ([t_l2] if t_l2 else [])
                        if ks == 0 and w4bg:
                            first.append(w4bg)
                        for f in range(f_lo, f_hi):
                            pr = min(128, HID0 - f * 128)
                            d = first if f == f_lo else []
                            emit("te", d, lambda g, f=f, pr=pr, f_lo=f_lo: g.matmul(
                                psum[4][:, 0:384],
                                stage16[0:pr, (f - f_lo) * 128:(f - f_lo) * 128 + 128],
                                bigT[0:pr, GW0 + f * 384:GW0 + (f + 1) * 384],
                                start=(f == 0), stop=(f == NF0 - 1)))
                        w4sg = tok("te")
                    t_mm = tok("te")
                    t_ev = emit("sc", [t_mm, tq_w2] + ([w4bg] if w4bg else []),
                                lambda g: g.activation(
                        stage32[:, 0:384], psum[4][:, 0:384], AF.Identity, scale=s2_col))
                    w4bg = emit("ve", [t_ev], lambda g, d128=d128: g.tensor_tensor(
                        bigT[:, XW10 + d128 * 384:XW10 + (d128 + 1) * 384],
                        bigT[:, XW10 + d128 * 384:XW10 + (d128 + 1) * 384],
                        stage32[:, 0:384], op=ALU.add))
                x1w_done = [tok("te"), tok("ve"), tok("sc")]
                if dbgw is not None:
                    x1w_done.append(dma("gp", list(x1w_done), dbgw[:, 12288:18432],
                                        bigT[:, XW10:XW10 + 6144]))

                sqg = None
                for kk in range(16):
                    dm1 = [*x1w_done] if kk == 0 else []
                    emit("te", dm1, lambda g, kk=kk: g.matmul(
                        psum[5][0:1, 0:384], onesc[:, 0:1],
                        bigT[:, XW10 + kk * 384:XW10 + (kk + 1) * 384],
                        start=(kk == 0), stop=(kk == 15)))
                t_mrow = tok("te")
                t_mr = emit("sc", [t_mrow], lambda g: g.activation(
                    stage32[0:1, 384:768], psum[5][0:1, 0:384], AF.Identity,
                    scale=float(1.0 / C)))
                for kk in range(16):
                    dsq = [*x1w_done] + ([sqg] if sqg else [])
                    t_sq = emit("ve", dsq, lambda g, kk=kk: g.tensor_tensor(
                        stage16[:, 0:384], bigT[:, XW10 + kk * 384:XW10 + (kk + 1) * 384],
                        bigT[:, XW10 + kk * 384:XW10 + (kk + 1) * 384], op=ALU.mult))
                    emit("te", [t_sq, t_mrow] if kk == 0 else [t_sq],
                         lambda g, kk=kk: g.matmul(
                        psum[3][0:1, 0:384], onesc[:, 0:1], stage16[:, 0:384],
                        start=(kk == 0), stop=(kk == 15)))
                    sqg = tok("te")
                t_vrow = tok("te")
                t_vr = emit("sc", [t_vrow], lambda g: g.activation(
                    stage32[0:1, 768:1152], psum[3][0:1, 0:384], AF.Identity,
                    scale=float(1.0 / C)))
                t_vr = emit("ve", [t_vr, t_mr], lambda g: g.tensor_tensor(
                    stage32[0:1, 0:384], stage32[0:1, 384:768], stage32[0:1, 384:768],
                    op=ALU.mult))
                t_vr = emit("ve", [t_vr], lambda g: g.tensor_tensor(
                    stage32[0:1, 768:1152], stage32[0:1, 768:1152], stage32[0:1, 0:384],
                    op=ALU.subtract))
                t_vr = emit("ve", [t_vr], lambda g: g.tensor_scalar(
                    stage32[0:1, 768:1152], stage32[0:1, 768:1152], float(LN_EPS), None,
                    op0=ALU.add))
                t_rw = rsqrt_newton(t_vr, stage32[0:1, 768:1152], stage32[0:1, 1152:1536],
                                    stage32[0:1, 0:384], stage32[0:1, 1536:1920])
                t_bm = emit("te", [t_mr, t_ones, tok("ve")], lambda g: g.matmul(
                    psum[5][0:128, 0:384], ones1r32[0:1, 0:128], stage32[0:1, 384:768],
                    start=True, stop=True))
                t_br = emit("te", [t_rw, t_ones], lambda g: g.matmul(
                    psum[3][0:128, 0:384], ones1r32[0:1, 0:128], stage32[0:1, 1152:1536],
                    start=True, stop=True))
                xnwg = None
                for kk in range(16):
                    dx1 = [t_bm, t_br] + ([xnwg] if xnwg else [])
                    t_t1 = emit("ve", dx1, lambda g, kk=kk: g.tensor_tensor(
                        stage32[:, 0:384], bigT[:, XW10 + kk * 384:XW10 + (kk + 1) * 384],
                        psum[5][0:128, 0:384], op=ALU.subtract))
                    xnwg = emit("ve", [t_t1], lambda g, kk=kk: g.tensor_tensor(
                        bigT[:, WX0 + kk * 384:WX0 + (kk + 1) * 384],
                        stage32[:, 0:384], psum[3][0:128, 0:384], op=ALU.mult))
                xn1w_done = [tok("te"), tok("ve"), tok("sc")]
                if dbgw is not None:
                    xn1w_done.append(dma("gp", list(xn1w_done), dbgw[:, 18432:24576],
                                        bigT[:, WX0:WX0 + 6144]))

                ipw_sg = {0: None, 1: None}
                ipw_bg = {0: None, 1: None}
                t_dtw = None
                for fi, f in enumerate(range(32, 66)):
                    sl = fi % 2
                    bank = psum[4] if sl == 0 else psum[1]
                    fw_ = 128 if f < 65 else 32
                    dl = ([ipw_sg[sl]] if ipw_sg[sl] else [tq_win[0], tq_win[1]]) + list(xn1w_done)
                    t_ld = dma("sy", dl, stage16[:, sl * 2048:sl * 2048 + 16 * fw_],
                               winq_d.rearrange("(k p) f -> p k f", p=128)[:, :, f * 128:f * 128 + fw_])
                    first = [t_ld, *xn1w_done] + ([ipw_bg[sl]] if ipw_bg[sl] else [])
                    for kk in range(16):
                        dd = first if kk == 0 else []
                        emit("te", dd, lambda g, kk=kk, sl=sl, fw_=fw_, bank=bank: g.matmul(
                            bank[0:fw_, 0:260],
                            stage16[:, sl * 2048 + kk * fw_:sl * 2048 + (kk + 1) * fw_],
                            bigT[:, WX0 + kk * 384 + 124:WX0 + kk * 384 + 384],
                            start=(kk == 0), stop=(kk == 15)))
                    t_mm = tok("te")
                    ipw_sg[sl] = t_mm
                    if f < 65:
                        dst = bigT[:, ZW0 + (f - 32) * 260:ZW0 + (f - 32) * 260 + 260]
                    else:
                        dst = stage32[0:32, 1024:1284]
                    t_ev = emit("sc", [t_mm, *xn1w_done], lambda g, dst=dst, bank=bank, fw_=fw_:
                                g.copy(dst, bank[0:fw_, 0:260]))
                    ipw_bg[sl] = t_ev
                    if f == 65:
                        t_dtw = t_ev
                t_ipw = tok("sc")

                t_sp = emit("sc", [t_dtw, t_db5], lambda g: g.activation(
                    stage32[0:32, 1284:1540], stage32[0:32, 1028:1284], AF.Exp,
                    bias=l1w[0:32, 167:168]))
                t_sp = emit("sc", [t_sp], lambda g: g.activation(
                    stage32[0:32, 1540:1796], stage32[0:32, 1284:1540], AF.Ln, bias=1.0))
                t_ca = emit("ve", [t_sp, t_an], lambda g: g.tensor_scalar(
                    stage32[0:32, 1284:1540], stage32[0:32, 1540:1796],
                    l1w[0:32, 166:167], None, op0=ALU.mult))
                srcb, dstb = 1284, 256
                for kstep in (1, 2, 4, 8, 16, 32, 64, 128):
                    t_c1 = emit("ve", [t_ca], lambda g, k=kstep, s=srcb, d=dstb: g.tensor_copy(
                        stage32[0:32, d:d + k], stage32[0:32, s:s + k]))
                    t_ca = emit("ve", [t_c1], lambda g, k=kstep, s=srcb, d=dstb: g.tensor_tensor(
                        stage32[0:32, d + k:d + 256], stage32[0:32, s + k:s + 256],
                        stage32[0:32, s:s + 256 - k], op=ALU.add))
                    srcb, dstb = dstb, srcb
                assert srcb == 1284
                t_dc = emit("ve", [t_ca], lambda g: g.tensor_scalar(
                    stage32[0:32, 256:512], stage32[0:32, 1284:1540],
                    stage32[0:32, 1539:1540], -1.0, op0=ALU.subtract, op1=ALU.mult))
                t_dc = emit("sc", [t_dc], lambda g: g.activation(
                    stage32[0:32, 256:512], stage32[0:32, 256:512], AF.Exp))
                wtr = [tok("sc")]
                for (srcbase, dstc, tdep) in ((1540, 512, t_sp), (256, 640, t_dc)):
                    for hs in range(2):
                        dtt = [tdep, t_id32, wtr[0]]
                        tt = emit("te", dtt, lambda g, srcbase=srcbase, hs=hs: g.transpose(
                            psum[6][0:128, 0:32],
                            stage32[0:32, srcbase + hs * 128:srcbase + hs * 128 + 128],
                            idsb32[0:32, 0:32]))
                        wtr[0] = emit("sc", [tt], lambda g, dstc=dstc, hs=hs: g.copy(
                            stage32[:, dstc + hs * 32:dstc + hs * 32 + 32],
                            psum[6][0:128, 0:32]))
                wprep = [tok("sc"), tok("ve")]

                def conv_tile_w(ct, dst16, deps):
                    base = ZW0 + ct * 260
                    t1 = emit("ve", list(deps) + [t_cw], lambda g, base=base, ct=ct: g.tensor_scalar(
                        stage32[:, 0:256], bigT[:, base + 1:base + 257],
                        l1w[:, ct * 4:ct * 4 + 1], None, op0=ALU.mult))
                    for jj in (1, 2, 3):
                        t1 = emit("ve", [t1], lambda g, base=base, ct=ct, jj=jj: g.scalar_tensor_tensor(
                            stage32[:, 0:256], bigT[:, base + 1 + jj:base + 257 + jj],
                            l1w[:, ct * 4 + jj:ct * 4 + jj + 1], stage32[:, 0:256],
                            op0=ALU.mult, op1=ALU.add))
                    return emit("sc", [t1, t_cb5], lambda g, ct=ct, dst16=dst16: g.activation(
                        dst16, stage32[:, 0:256], AF.Silu, bias=l1w[:, 132 + ct:133 + ct]))

                t_cbw = conv_tile_w(32, bigT[:, CBW:CBW + 256], [t_ipw, *wprep])
                WHG = [t_cbw, *wprep, tok("te")]
                rw_toks = []
                for h in range(NHEADS):
                    t_xh = conv_tile_w(h, stage16[:, 0:256], [t_ipw, *WHG])
                    t_x = []
                    for hs in range(2):
                        ttr = transpose128r(t_xh, stage16[:, hs * 128:(hs + 1) * 128],
                                            stage16[:, 1024 + hs * 128:1152 + hs * 128])
                        t_x.append(emit("ve", [ttr, *wprep], lambda g, hs=hs, h=h: g.tensor_scalar(
                            stage16[:, 1024 + hs * 128:1152 + hs * 128],
                            stage16[:, 1024 + hs * 128:1152 + hs * 128],
                            stage32[:, 512 + hs * 32 + h:512 + hs * 32 + h + 1],
                            None, op0=ALU.mult)))
                    t_b = []
                    for hs in range(2):
                        ttr = transpose128r(t_cbw, bigT[0:64, CBW + hs * 128:CBW + (hs + 1) * 128],
                                            stage16[:, 1280 + hs * 64:1344 + hs * 64])
                        t_b.append(emit("ve", [ttr, *wprep], lambda g, hs=hs, h=h: g.tensor_scalar(
                            stage16[:, 1280 + hs * 64:1344 + hs * 64],
                            stage16[:, 1280 + hs * 64:1344 + hs * 64],
                            stage32[:, 640 + hs * 32 + h:640 + hs * 32 + h + 1],
                            None, op0=ALU.mult)))
                    for hs in range(2):
                        emit("te", [t_b[hs], t_x[hs], *WHG] if hs == 0 else [t_b[1], t_x[1]],
                             lambda g, hs=hs: g.matmul(
                            psum[2][0:64, 0:128],
                            stage16[:, 1280 + hs * 64:1344 + hs * 64],
                            stage16[:, 1024 + hs * 128:1152 + hs * 128],
                            start=(hs == 0), stop=(hs == 1)))
                    t_stm = tok("te")
                    t_cpr = emit("sc", [t_stm, *WHG], lambda g: g.copy(
                        stage32[0:64, 1796:1924], psum[2][0:64, 0:128]))
                    rw_toks.append(dma("gp", [t_cpr], rw_d[0:64, h * 128:(h + 1) * 128],
                                       stage32[0:64, 1796:1924]))
                    WHG = [tok("te"), tok("ve"), tok("sc"), rw_toks[-1]]
                t_hal = dma("gp", [t_ipw, tok("ve"), tok("sc")], halo_d[:],
                            bigT[:, ZW0:ZW0 + 33 * 260].rearrange(
                                "p (a b) -> p a b", b=260)[:, :, 256:260])
                win_done = [tok("te"), tok("ve"), tok("sc"), t_hal,
                            P.ring_join("gp"), P.ring_join("sy")]
            else:
                win_done = []

        NF0 = ceil_div(HID0, 128)

        def bigT_f(f):
            return bigT[:, f * TQ:(f + 1) * TQ]

        if stage >= 2:
            fch = [(c0, min(256, HID0 - c0)) for c0 in range(0, HID0, 256)]
            strip_guard = None
            evac_guard = {}
            for (c0, w) in fch:
                nk = 16
                d1 = [tq_w1, quant_done] + (list(win_done) if USE_WIN else []) + (
                    [strip_guard] if strip_guard else
                    (list(attn_done) + [*xnT_done] if stage >= 3 else [*xnT_done]))
                t_s1 = dma("sy", d1, stage16[:, 0:nk * w],
                           w1q_d.rearrange("(k p) h -> p k h", p=128)[:, :, c0:c0 + w])
                for t in range(NT):
                    bank = psum[t // 2]
                    off = (t % 2) * 256
                    first_of_bank = (t % 2 == 0)
                    d = [t_s1, *xnT_done, tok("sc"), tok("ve")] if t == 0 else []
                    for kk in range(16):
                        dd = d if kk == 0 else []
                        emit("te", dd, lambda g, kk=kk, t=t, w=w, bank=bank, off=off,
                             fob=first_of_bank: g.matmul(
                            bank[:, off:off + w],
                            xnT_d(kk)[:, t * 128:(t + 1) * 128],
                            stage16[:, kk * w:(kk + 1) * w],
                            start=(kk == 0 and fob), stop=(kk == 15 and not fob)))
                t_h1 = tok("te")
                for t in range(NT):
                    bank = psum[t // 2]
                    off = (t % 2) * 256
                    emit("sc", [t_h1], lambda g, bank=bank, off=off, t=t, w=w: g.activation(
                        stage32[:, t * 256:t * 256 + w], bank[:, off:off + w], AF.Silu,
                        scale=s1_col))
                t_silu = tok("sc")
                t_s3 = dma("sy", [tq_w3, t_h1], stage16[:, 0:nk * w],
                           w3q_d.rearrange("(k p) h -> p k h", p=128)[:, :, c0:c0 + w])
                for t in range(NT):
                    bank = psum[t // 2]
                    off = (t % 2) * 256
                    first_of_bank = (t % 2 == 0)
                    d = [t_s3, t_silu] if t == 0 else []
                    for kk in range(16):
                        dd = d if kk == 0 else []
                        emit("te", dd, lambda g, kk=kk, t=t, w=w, bank=bank, off=off,
                             fob=first_of_bank: g.matmul(
                            bank[:, off:off + w],
                            xnT_d(kk)[:, t * 128:(t + 1) * 128],
                            stage16[:, kk * w:(kk + 1) * w],
                            start=(kk == 0 and fob), stop=(kk == 15 and not fob)))
                t_h3 = tok("te")
                for t in range(NT):
                    bank = psum[t // 2]
                    off = (t % 2) * 256
                    sl = (t % 4) * 256
                    dg = [t_h3]
                    if evac_guard.get(sl):
                        dg.append(evac_guard[sl])
                    t_g = emit("ve", dg, lambda g, bank=bank, off=off, sl=sl, t=t, w=w:
                               g.scalar_tensor_tensor(
                                   evac16[:, sl:sl + w], bank[:, off:off + w], s3_col,
                                   stage32[:, t * 256:t * 256 + w],
                                   op0=ALU.mult, op1=ALU.mult))
                    last_tt = None
                    for half in range(ceil_div(w, 128)):
                        hw = min(128, w - half * 128)
                        ft = c0 // 128 + half
                        last_tt = transpose128r(
                            t_g, evac16[:, sl + half * 128:sl + half * 128 + hw],
                            bigT_f(ft)[0:hw, t * 128:(t + 1) * 128])
                    evac_guard[sl] = last_tt
                strip_guard = tok("te")

            gT_done = [tok("te"), tok("sc")]
            tr32_guard = attn_done[0] if stage >= 3 else tok("sc")
            w2_strip_guard = None
            w2_bank_guard = {0: None, 1: None}
            KSPLIT = 22
            for d128 in range(16):
                t_evs = {}
                for ks in range(2):
                    f_lo = 0 if ks == 0 else KSPLIT
                    f_hi = KSPLIT if ks == 0 else NF0
                    dstrip = [tq_w2, tok("te")]
                    if w2_strip_guard:
                        dstrip.append(w2_strip_guard)
                    full_end = min(f_hi, 42)
                    t_l1 = dma("sy", dstrip,
                               stage16[:, 0:(full_end - f_lo) * 128],
                               w2q_d[f_lo * 128:full_end * 128, :].rearrange(
                                   "(f p) c -> p f c", p=128)[:, :, d128 * 128:(d128 + 1) * 128])
                    t_l2 = None
                    if f_hi == NF0:
                        t_l2 = dma("sy", [], stage16[0:85, (42 - f_lo) * 128:(43 - f_lo) * 128],
                                   w2q_d[5376:5461, d128 * 128:(d128 + 1) * 128])
                    for th in range(2):
                        bank = psum[4 + th]
                        first = [t_l1, *gT_done] + ([t_l2] if t_l2 else [])
                        if ks == 0 and w2_bank_guard[th]:
                            first.append(w2_bank_guard[th])
                        for f in range(f_lo, f_hi):
                            pr = min(128, HID0 - f * 128)
                            d = first if f == f_lo else []
                            emit("te", d, lambda g, f=f, pr=pr, bank=bank, f_lo=f_lo, th=th: g.matmul(
                                bank[:, 0:512],
                                stage16[0:pr, (f - f_lo) * 128:(f - f_lo) * 128 + 128],
                                bigT_f(f)[0:pr, th * 512:(th + 1) * 512],
                                start=(f == 0), stop=(f == NF0 - 1)))
                    w2_strip_guard = tok("te")
                for th in range(2):
                    bank = psum[4 + th]
                    t_mm = tok("te")
                    dv = [t_mm, tr32_guard]
                    t_ev = emit("sc", dv, lambda g, bank=bank: g.activation(
                        stage32[:, 1024:1536], bank[:, 0:512], AF.Identity, scale=s2_col))
                    w2_bank_guard[th] = t_ev
                    for j in range(4):
                        t = th * 4 + j
                        tt = emit("te", [t_ev, t_id32, tr32_guard], lambda g, j=j: g.transpose(
                            psum[6][:, 128:256],
                            stage32[:, 1024 + j * 128:1024 + j * 128 + 128],
                            idsb32[:]))
                        ta = emit("ve", [tt], lambda g, t=t, d128=d128: g.tensor_tensor(
                            xres(t)[:, d128 * 128:(d128 + 1) * 128],
                            xres(t)[:, d128 * 128:(d128 + 1) * 128],
                            psum[6][:, 128:256], op=ALU.add))
                        tr32_guard = ta
                    t_evs["prev"] = tr32_guard
            ffn0_done = [tok("ve")]

        if stage >= 4:
            ln1_mean, ln1_rstd, ln1_var = [], [], []
            for i in range(NT):
                mc, _ = sm_col(); rc, _ = sm_col(); vc, _ = sm_col()
                ln1_mean.append(mc); ln1_rstd.append(rc); ln1_var.append(vc)
            l1_entry = [*ffn0_done, tok("te"), tok("sc")]
            war = P.ring_join("sy")
            war = l1_entry[1]
            for i in range(NT):
                dsts = [xnT_d(d)[:, i * 128:(i + 1) * 128] for d in range(16)]
                tm_, tr_, t2_ = None, None, None
                tm_, tr_, t2_ = ln_stats([*ffn0_done][0] if i == 0 else war, xres(i),
                                         ln1_mean[i], ln1_rstd[i], LN_EPS, psum[6])
                emit("ve", [t2_], lambda g, i=i: g.tensor_copy(ln1_var[i], sm[:, 89:90]))
                dep = [tm_, tr_] + ([war] if war else [])
                t_ap = emit("ve", dep, lambda g, i=i: g.tensor_scalar(
                    stage16[:, 0:C], xres(i), ln1_mean[i], ln1_rstd[i],
                    op0=ALU.subtract, op1=ALU.mult))
                for d in range(16):
                    transpose128r(t_ap, stage16[:, d * 128:(d + 1) * 128], dsts[d])
                war = tok("te")
            xn1T_done = [tok("te"), tok("sc")]

            ff_r2 = []
            for i in range(NT):
                rc, _ = sm_col()
                ff_r2.append(rc)
            t_r2 = None
            for i in range(NT):
                ta = emit("ve", [xn1T_done[0]], lambda g, i=i: g.tensor_scalar(
                    sm[:, 16:17], ln1_var[i], float(LN_EPS), None, op0=ALU.add))
                ta = emit("ve", [ta], lambda g: g.reciprocal(sm[:, 16:17], sm[:, 16:17]))
                tb = emit("ve", [ta], lambda g, i=i: g.tensor_tensor(
                    sm[:, 17:18], ln1_var[i], sm[:, 16:17], op=ALU.mult))
                tc_ = emit("ve", [tb, constdep(RMS_EPS_FFN)], lambda g: g.tensor_tensor(
                    sm[:, 18:19], sm[:, 17:18], constcol(RMS_EPS_FFN), op=ALU.add))
                t_r2 = rsqrt_newton(tc_, sm[:, 18:19], ff_r2[i], sm[:, 19:20], sm[:, 28:29])

            ff_e2 = []
            for i in range(NT):
                ec, _ = sm_col()
                ff_e2.append(ec)
            xq_guard = None
            for i in range(NT):
                dd = [t_r2] + ([xq_guard] if xq_guard else [])
                t_xn = emit("ve", dd, lambda g, i=i: g.tensor_scalar(
                    stage32[:, 0:C], xres(i), ln1_mean[i], ln1_rstd[i],
                    op0=ALU.subtract, op1=ALU.mult))
                tg_prev = t_xn
                gam_st = i32st.bitcast(F32)
                for cch in range(4):
                    t_gl = dma("sy", [tg_prev], gam_st[:, 0:512],
                               gamma_rep[:, cch * 512:(cch + 1) * 512])
                    tg_prev = emit("ve", [t_gl, tg_prev], lambda g, cch=cch, gam_st=gam_st: g.tensor_tensor(
                        stage32[:, cch * 512:(cch + 1) * 512],
                        stage32[:, cch * 512:(cch + 1) * 512],
                        gam_st[:, 0:512], op=ALU.mult))
                t_m = emit("ve", [tg_prev], lambda g: g.tensor_reduce(
                    sm[:, 20:21], stage32[:, 0:C], axis=AX.X, op=ALU.max,
                    apply_absolute_value=True))
                t_mg = emit("ve", [t_m], lambda g, i=i: g.tensor_tensor(
                    sm[:, 21:22], sm[:, 20:21], ff_r2[i], op=ALU.mult))
                t_mg = emit("ve", [t_mg], lambda g: g.tensor_scalar(
                    sm[:, 21:22], sm[:, 21:22], 1e-5, None, op0=ALU.max))
                t_ct = emit("ve", [t_mg], lambda g: g.reciprocal(sm[:, 23:24], sm[:, 21:22]))
                t_ct = emit("ve", [t_ct, t_r2], lambda g, i=i: g.tensor_tensor(
                    sm[:, 22:23], sm[:, 23:24], ff_r2[i], op=ALU.mult))
                t_ct = emit("ve", [t_ct], lambda g: g.tensor_scalar(
                    sm[:, 22:23], sm[:, 22:23], 127.0, None, op0=ALU.mult))
                t_e2 = emit("ve", [t_mg, tq_f1], lambda g, i=i: g.scalar_tensor_tensor(
                    ff_e2[i], sm[:, 21:22], float(1.0 / 127.0), sf1_col,
                    op0=ALU.mult, op1=ALU.mult))
                tq_prev = tok("te")
                for cch in range(4):
                    dq = [t_ct] + ([tq_prev] if tq_prev else [])
                    t_i = emit("ve", dq, lambda g, cch=cch: g.tensor_scalar(
                        i32st[:, 0:512], stage32[:, cch * 512:(cch + 1) * 512],
                        sm[:, 22:23], None, op0=ALU.mult))
                    t_b = emit("ve", [t_i], lambda g, cch=cch: g.tensor_copy(
                        evac16[:, 0:512], i32st[:, 0:512]))
                    for dd4 in range(4):
                        d = cch * 4 + dd4
                        ttr = transpose128r(t_b, evac16[:, dd4 * 128:(dd4 + 1) * 128],
                                            stage16[:, d * 128:(d + 1) * 128])
                    tq_prev = tok("te")
                t_sx = dma("gp", [ttr, tok("sc")],
                           xq_d.rearrange("(a p) t -> p a t", p=128)[:, :, i * 128:(i + 1) * 128],
                           stage16.rearrange("p (a b) -> p a b", b=128)[:, 0:16, :])
                xq_guard = t_sx
            xq_done = [P.ring_join("gp"), tok("ve"), tok("sc"), tok("te")]

        if stage >= 5:
            CV0, CB0, GB0, GN0, TRI0 = 8192, 16772, 17028, 25220, 41604
            bigf = bigT.bitcast(F32)
            R0F = 16706
            GMP0, RS0 = 21058, 21090
            CT0 = 43654
            l1_entry = list(xq_done)

            def R_ap(h):
                return bigf[0:64, R0F + h * 128:R0F + (h + 1) * 128]

            t_tri5 = dma("sy", l1_entry, bigT[:, TRI0:TRI0 + 512], trineg[:])
            if dbg16 is not None and "dbgtri" in taps:
                t_tri5 = dma("gp", [t_tri5], dbg16[:, 0:512], bigT[:, TRI0:TRI0 + 512])
            if USE_WIN:
                P.begin_switch(4, qsb[0:1, 0:1], t_q)
                for body in range(4):
                    P.begin_body(body)
                    if body == 0:
                        emit("ve", l1_entry, lambda g: g.memset(bigT[:, 33412:41604], 0.0))
                        for ct in range(33):
                            emit("ve", [], lambda g, ct=ct: g.memset(
                                bigT[:, CV0 + ct * 260:CV0 + ct * 260 + 4], 0.0))
                    else:
                        dma("sy", [*l1_entry, *win_done],
                            bigf[0:64, R0F:R0F + NHEADS * 128], rw_d[:])
                        emit("ve", l1_entry, lambda g: g.memset(
                            bigT[64:128, 33412:41604], 0.0))
                        dma("sy", [*l1_entry, *win_done],
                            bigT[:, CV0:CV0 + 33 * 260].rearrange(
                                "p (a b) -> p a b", b=260)[:, :, 0:4],
                            halo_d.rearrange("p (a b) -> p a b", b=4))
                    P.end_body()
                sw3 = P.end_switch()
                t_init = [*sw3]
            else:
                t_r2z = emit("ve", l1_entry, lambda g: g.memset(bigT[:, 33412:41604], 0.0))
                t_hz = []
                for ct in range(33):
                    t_hz.append(emit("ve", l1_entry, lambda g, ct=ct: g.memset(
                        bigT[:, CV0 + ct * 260:CV0 + ct * 260 + 4], 0.0)))
                t_init = [t_r2z, t_hz[-1]]

            blk_gate = [*l1_entry, t_tri5, t_cw, t_cb5, t_an, t_db5, t_nw,
                        t_dd5, *t_init, tok("ve")]
            l1tr = [None]
            ip_strip_guard = {0: None, 1: None}
            ip_bank_guard = {0: None, 1: None}

            def conv_tile(ct, dst16, deps):
                base = CV0 + ct * 260
                t1 = emit("ve", list(deps) + [t_cw], lambda g, ct=ct, base=base: g.tensor_scalar(
                    stage32[:, 0:256], bigT[:, base + 1:base + 257],
                    l1w[:, ct * 4:ct * 4 + 1], None, op0=ALU.mult))
                for j in (1, 2, 3):
                    t1 = emit("ve", [t1], lambda g, ct=ct, base=base, j=j: g.scalar_tensor_tensor(
                        stage32[:, 0:256], bigT[:, base + 1 + j:base + 257 + j],
                        l1w[:, ct * 4 + j:ct * 4 + j + 1], stage32[:, 0:256],
                        op0=ALU.mult, op1=ALU.add))
                t2 = emit("sc", [t1, t_cb5], lambda g, ct=ct, dst16=dst16: g.activation(
                    dst16, stage32[:, 0:256], AF.Silu, bias=l1w[:, 132 + ct:133 + ct]))
                return t2

            for ci in range(4):
                t_dt_ev = None
                for f in range(66):
                    sl = f % 2
                    bank = psum[4] if sl == 0 else psum[1]
                    fw = 128 if f < 65 else 32
                    dl = ([ip_strip_guard[sl]] if ip_strip_guard[sl]
                          else [tq_win[0], tq_win[1]]) + list(blk_gate)
                    t_ld = dma("sy", dl, stage16[:, sl * 2048:sl * 2048 + 16 * fw],
                               winq_d.rearrange("(k p) f -> p k f", p=128)[:, :, f * 128:f * 128 + fw])
                    first = [t_ld, *blk_gate] + ([ip_bank_guard[sl]] if ip_bank_guard[sl] else [])
                    for kk in range(16):
                        dd = first if kk == 0 else []
                        emit("te", dd, lambda g, kk=kk, sl=sl, fw=fw, bank=bank, ci=ci: g.matmul(
                            bank[0:fw, 0:256],
                            stage16[:, sl * 2048 + kk * fw:sl * 2048 + (kk + 1) * fw],
                            xnT_d(kk)[:, ci * 256:(ci + 1) * 256],
                            start=(kk == 0), stop=(kk == 15)))
                    t_mm = tok("te")
                    ip_strip_guard[sl] = t_mm
                    if f < 32:
                        dst = bigT[:, f * 256:(f + 1) * 256]
                    elif f < 65:
                        ct = f - 32
                        dst = bigT[:, CV0 + ct * 260 + 4:CV0 + ct * 260 + 260]
                    else:
                        dst = stage32[0:32, 1024:1280]
                    t_ev = emit("sc", [t_mm, *blk_gate], lambda g, dst=dst, bank=bank, fw=fw:
                                g.copy(dst, bank[0:fw, 0:256]))
                    ip_bank_guard[sl] = t_ev
                    if f == 65:
                        t_dt_ev = t_ev
                t_ip = tok("sc")

                t_sp = emit("sc", [t_dt_ev, t_db5], lambda g: g.activation(
                    stage32[0:32, 1280:1536], stage32[0:32, 1024:1280], AF.Exp,
                    bias=l1w[0:32, 167:168]))
                t_sp = emit("sc", [t_sp], lambda g: g.activation(
                    stage32[0:32, 1024:1280], stage32[0:32, 1280:1536], AF.Ln, bias=1.0))
                t_ca = emit("ve", [t_sp, t_an], lambda g: g.tensor_scalar(
                    stage32[0:32, 1280:1536], stage32[0:32, 1024:1280],
                    l1w[0:32, 166:167], None, op0=ALU.mult))
                srcb, dstb = 1280, 1536
                for kstep in (1, 2, 4, 8, 16, 32, 64, 128):
                    t_c1 = emit("ve", [t_ca], lambda g, k=kstep, s=srcb, d=dstb: g.tensor_copy(
                        stage32[0:32, d:d + k], stage32[0:32, s:s + k]))
                    t_ca = emit("ve", [t_c1], lambda g, k=kstep, s=srcb, d=dstb: g.tensor_tensor(
                        stage32[0:32, d + k:d + 256], stage32[0:32, s + k:s + 256],
                        stage32[0:32, s:s + 256 - k], op=ALU.add))
                    srcb, dstb = dstb, srcb
                assert srcb == 1280
                t_dc = emit("ve", [t_ca], lambda g: g.tensor_scalar(
                    stage32[0:32, 1536:1792], stage32[0:32, 1280:1536],
                    stage32[0:32, 1535:1536], -1.0, op0=ALU.subtract, op1=ALU.mult))
                t_dc = emit("sc", [t_dc], lambda g: g.activation(
                    stage32[0:32, 1536:1792], stage32[0:32, 1536:1792], AF.Exp))
                t_gm = emit("sc", [t_ca, *blk_gate], lambda g: g.activation(
                    l1w[0:32, 168:169], stage32[0:32, 1535:1536], AF.Exp))
                for (srcbase, dstc, tdep) in ((1024, 512, t_sp), (1280, 576, t_ca),
                                              (1536, 640, t_dc)):
                    for hs in range(2):
                        dtt = [tdep, t_id32] + ([l1tr[0]] if l1tr[0] else list(blk_gate))
                        tt = emit("te", dtt, lambda g, srcbase=srcbase, hs=hs: g.transpose(
                            psum[6][0:128, 0:32],
                            stage32[0:32, srcbase + hs * 128:srcbase + hs * 128 + 128],
                            idsb32[0:32, 0:32]))
                        l1tr[0] = emit("sc", [tt], lambda g, dstc=dstc, hs=hs: g.copy(
                            stage32[:, dstc + hs * 32:dstc + hs * 32 + 32],
                            psum[6][0:128, 0:32]))
                t_ecT = None
                for hs in range(2):
                    t_ecT = emit("sc", [l1tr[0]], lambda g, hs=hs: g.activation(
                        stage32[:, 704 + hs * 32:736 + hs * 32],
                        stage32[:, 576 + hs * 32:608 + hs * 32], AF.Exp))
                ttg = emit("te", [t_gm, t_id32] + ([l1tr[0]] if l1tr[0] else []),
                           lambda g: g.transpose(psum[6][0:1, 0:32], l1w[0:32, 168:169],
                                                 idsb32[0:32, 0:32]))
                l1tr[0] = emit("sc", [ttg], lambda g: g.copy(
                    bigf[0:1, GMP0:GMP0 + 32], psum[6][0:1, 0:32]))
                t_gmr = l1tr[0]
                t_prep = [tok("sc"), tok("ve")]

                t_cbc = conv_tile(32, bigT[:, CB0:CB0 + 256], [t_ip, *blk_gate])
                t_ct0 = dma("sy", [t_cbc, *blk_gate], bigT[0:64, CT0:CT0 + 256],
                            bigT[64:128, CB0:CB0 + 256])

                HG = [t_cbc, t_ct0, *t_prep, tok("te")]
                for h in range(NHEADS):
                    p0_ = (h % 2) * 64
                    cb_ = (h // 2) * 128
                    t_xh = conv_tile(h, stage16[:, 0:256], [t_ip, *HG])
                    t_bl = []
                    for j in range(2):
                        dr = [t_xh, *HG] if j == 0 else [tok("te")]
                        t_rep = emit("ve", dr, lambda g, j=j, h=h: g.tensor_scalar(
                            stage32[:, 0:128], idsb32[:], 0.0,
                            stage32[:, 576 + j * 32 + h:576 + j * 32 + h + 1],
                            op0=ALU.mult, op1=ALU.add))
                        tt = emit("te", [t_rep, t_id32] + ([l1tr[0]] if l1tr[0] else []),
                                  lambda g: g.transpose(psum[6][0:128, 0:128],
                                                        stage32[:, 0:128], idsb32[:]))
                        l1tr[0] = emit("sc", [tt], lambda g, j=j: g.copy(
                            stage32[:, 256 + j * 128:256 + (j + 1) * 128],
                            psum[6][0:128, 0:128]))
                        t_bl.append(l1tr[0])
                    t_ce = None
                    for j in range(2):
                        t_rep = emit("ve", [t_ecT, tok("te")], lambda g, j=j, h=h: g.tensor_scalar(
                            stage32[:, 0:128], idsb32[:], 0.0,
                            stage32[:, 704 + j * 32 + h:704 + j * 32 + h + 1],
                            op0=ALU.mult, op1=ALU.add))
                        tt = emit("te", [t_rep, t_id32] + ([l1tr[0]] if l1tr[0] else []),
                                  lambda g: g.transpose(psum[6][0:128, 0:128],
                                                        stage32[:, 0:128], idsb32[:]))
                        t_ce = emit("ve", [tt, t_ct0], lambda g, j=j: g.tensor_tensor(
                            stage16[0:64, 1408 + j * 128:1536 + j * 128],
                            bigT[0:64, CT0 + j * 128:CT0 + (j + 1) * 128],
                            psum[6][0:64, 0:128], op=ALU.mult))
                        l1tr[0] = t_ce
                    t_x = []
                    for hs in range(2):
                        ttr = transpose128r(t_xh, stage16[:, hs * 128:(hs + 1) * 128],
                                            stage16[:, 1024 + hs * 128:1152 + hs * 128])
                        t_x.append(emit("ve", [ttr, *t_prep], lambda g, hs=hs, h=h: g.tensor_scalar(
                            stage16[:, 1024 + hs * 128:1152 + hs * 128],
                            stage16[:, 1024 + hs * 128:1152 + hs * 128],
                            stage32[:, 512 + hs * 32 + h:512 + hs * 32 + h + 1],
                            None, op0=ALU.mult)))
                    t_b = []
                    for hs in range(2):
                        ttr = transpose128r(t_cbc, bigT[0:64, CB0 + hs * 128:CB0 + (hs + 1) * 128],
                                            stage16[:, 1280 + hs * 64:1344 + hs * 64])
                        t_b.append(emit("ve", [ttr, *t_prep], lambda g, hs=hs, h=h: g.tensor_scalar(
                            stage16[:, 1280 + hs * 64:1344 + hs * 64],
                            stage16[:, 1280 + hs * 64:1344 + hs * 64],
                            stage32[:, 640 + hs * 32 + h:640 + hs * 32 + h + 1],
                            None, op0=ALU.mult)))
                    t_g0 = emit("te", [t_cbc, t_ct0, *HG], lambda g: g.matmul(
                        psum[0][:, 0:256], bigT[0:64, CB0:CB0 + 128],
                        bigT[0:64, CT0:CT0 + 256], start=True, stop=False))
                    t_g1 = emit("te", [], lambda g: g.matmul(
                        psum[0][:, 256:512], bigT[0:64, CB0 + 128:CB0 + 256],
                        bigT[0:64, CT0:CT0 + 256], start=False, stop=True))
                    tg_ = (t_g1, t_g1)
                    t_df_last, t_pp = None, []
                    for hs in range(2):
                        ddf = [t_bl[1], t_tri5, tok("te")] + ([t_df_last] if t_df_last else [])
                        t_df = emit("ve", ddf, lambda g, hs=hs, h=h: g.scalar_tensor_tensor(
                            stage32[:, 0:256], stage32[:, 256:512],
                            stage32[:, 576 + hs * 32 + h:576 + hs * 32 + h + 1],
                            bigT[:, TRI0 + hs * 256:TRI0 + (hs + 1) * 256],
                            op0=ALU.subtract, op1=ALU.add))
                        t_ex = emit("sc", [t_df] + ([t_pp[0]] if hs == 1 else []),
                                    lambda g: g.activation(stage16[:, 256:512],
                                                           stage32[:, 0:256], AF.Exp))
                        t_df_last = t_ex
                        t_pp.append(emit("ve", [t_ex, tg_[hs]], lambda g, hs=hs: g.tensor_tensor(
                            stage16[:, 512 + hs * 256:768 + hs * 256], stage16[:, 256:512],
                            psum[0][:, hs * 256:(hs + 1) * 256], op=ALU.mult)))
                        emit("te", [t_pp[hs], t_x[hs], *HG], lambda g, hs=hs: g.matmul(
                            psum[1][:, 0:256], stage16[:, 1024 + hs * 128:1152 + hs * 128],
                            stage16[:, 512 + hs * 256:768 + hs * 256],
                            start=(hs == 0), stop=False))
                        emit("te", [t_b[hs], t_x[hs]] if hs == 0 else [],
                             lambda g, hs=hs: g.matmul(
                            psum[2][0:64, 0:128],
                            stage16[:, 1280 + hs * 64:1344 + hs * 64],
                            stage16[:, 1024 + hs * 128:1152 + hs * 128],
                            start=(hs == 0), stop=(hs == 1)))
                    t_st = tok("te")
                    dbg_toks = []
                    if dbg32 is not None and "dbgtri" not in taps and ci == 3 and h == NHEADS - 1:
                        dd_ = [t_st, tok("ve"), tok("sc")]
                        for (dst, src) in ((dbg32[:, 0:256], stage32[:, 256:512]),
                                           (dbg32[:, 256:512], stage32[:, 0:256]),
                                           (dbg32[:, 512:768], stage32[:, 512:768]),
                                           (dbg32[:, 768:1024], stage32[:, 768:1024]),
                                           (dbg16[:, 0:256], stage16[:, 256:512]),
                                           (dbg16[:, 256:512], stage16[:, 512:768]),
                                           (dbg16[:, 512:768], stage16[:, 768:1024]),
                                           (dbg16[:, 768:1024], stage16[:, 0:256])):
                            dbg_toks.append(dma("gp", dd_, dst, src))
                    t_rb = emit("ve", [*HG, *dbg_toks], lambda g, h=h: g.tensor_copy(
                        stage16[0:64, 1664:1792], R_ap(h)))
                    t_yo = emit("te", [t_rb, t_ce], lambda g: g.matmul(
                        psum[1][:, 0:256], stage16[0:64, 1664:1792],
                        stage16[0:64, 1408:1664], start=False, stop=True))
                    t_gb = emit("te", [t_gmr, t_pp[0], t_pp[1], t_ones, *dbg_toks],
                                lambda g, h=h: g.matmul(
                        psum[0][0:64, 0:1], ones1r32[0:1, 0:64],
                        bigf[0:1, GMP0 + h:GMP0 + h + 1], start=True, stop=True))
                    t_gc = emit("sc", [t_gb], lambda g: g.copy(
                        sm[0:64, 34:35], psum[0][0:64, 0:1]))
                    t_ru = emit("ve", [t_gc, t_st, t_rb], lambda g, h=h:
                                g.scalar_tensor_tensor(
                        R_ap(h), R_ap(h),
                        sm[0:64, 34:35], psum[2][0:64, 0:128],
                        op0=ALU.mult, op1=ALU.add))
                    t_y = emit("ve", [t_yo, t_dd5, t_df_last], lambda g, h=h:
                               g.scalar_tensor_tensor(
                        stage32[:, 256:512], stage16[:, 0:256], l1w[:, 201 + h:202 + h],
                        psum[1][:, 0:256], op0=ALU.mult, op1=ALU.add))
                    t_sz = emit("sc", [t_ip, *HG], lambda g, h=h: g.activation(
                        stage16[:, 1792:2048], bigT[:, h * 256:(h + 1) * 256], AF.Silu))
                    t_g = emit("ve", [t_y, t_sz, *blk_gate], lambda g, h=h: g.tensor_tensor(
                        bigT[:, GB0 + h * 256:GB0 + (h + 1) * 256], stage32[:, 256:512],
                        stage16[:, 1792:2048], op=ALU.mult))
                    t_g2 = emit("ve", [t_g], lambda g, h=h: g.tensor_tensor(
                        stage16[:, 2048:2304], bigT[:, GB0 + h * 256:GB0 + (h + 1) * 256],
                        bigT[:, GB0 + h * 256:GB0 + (h + 1) * 256], op=ALU.mult))
                    emit("te", [t_g2, t_ones, t_ru], lambda g, h=h: g.matmul(
                        psum[5][0:1, 0:256], onesc[:, 0:1], stage16[:, 2048:2304],
                        start=(h == 0), stop=(h == NHEADS - 1)))
                    HG = [tok("te"), tok("ve"), tok("sc"), *dbg_toks]

                t_r1 = emit("sc", [*HG], lambda g: g.activation(
                    stage32[0:1, 0:256], psum[5][0:1, 0:256], AF.Identity,
                    scale=float(1.0 / D_INNER)))
                t_r2m = emit("ve", [t_r1], lambda g: g.tensor_scalar(
                    stage32[0:1, 0:256], stage32[0:1, 0:256], float(RMS_EPS_ATTN),
                    None, op0=ALU.add))
                t_rs3 = rsqrt_newton(t_r2m, stage32[0:1, 0:256], bigf[0:1, RS0:RS0 + 256],
                                     stage32[0:1, 256:512], stage32[0:1, 768:1024])
                t_rb3 = emit("te", [t_rs3, t_ones, *HG], lambda g: g.matmul(
                    psum[3][0:128, 0:256], ones1r32[0:1, 0:128],
                    bigf[0:1, RS0:RS0 + 256], start=True, stop=True))
                for h2 in range(NHEADS):
                    emit("ve", [t_rb3, t_nw, *blk_gate], lambda g, h2=h2:
                         g.scalar_tensor_tensor(
                        bigT[:, GN0 + h2 * 256:GN0 + (h2 + 1) * 256],
                        bigT[:, GB0 + h2 * 256:GB0 + (h2 + 1) * 256],
                        l1w[:, 169 + h2:170 + h2], psum[3][0:128, 0:256],
                        op0=ALU.mult, op1=ALU.mult))
                t_gnd = tok("ve")
                for ct in range(33):
                    emit("ve", [t_gnd] if ct == 0 else [], lambda g, ct=ct: g.tensor_copy(
                        bigT[:, CV0 + ct * 260:CV0 + ct * 260 + 4],
                        bigT[:, CV0 + ct * 260 + 256:CV0 + ct * 260 + 260]))
                if dbg16 is not None and "dbgtri" in taps and ci == 0:
                    dma("gp", [tok("ve"), tok("sc"), tok("te")],
                        dbg16[:, 512:1024], bigT[:, TRI0:TRI0 + 512])
                blk_gate = [tok("te"), tok("ve"), tok("sc")]

                if "wout" in DEV_SKIP:
                    continue
                wo_strip_guard = None
                wo_bank_guard = None
                for d128 in range(16):
                    dstrip = list(blk_gate) + ([wo_strip_guard] if wo_strip_guard else [])
                    t_ws = dma("sy", dstrip, stage16[:, 0:4096],
                               woq_d.rearrange("(k p) d -> p k d", p=128)[:, :, d128 * 128:(d128 + 1) * 128])
                    first = [t_ws, tq_win[0], tq_win[1], *blk_gate] + (
                        [wo_bank_guard] if wo_bank_guard else [])
                    for kk in range(32):
                        dd = first if kk == 0 else []
                        emit("te", dd, lambda g, kk=kk: g.matmul(
                            psum[4][:, 0:256], stage16[:, kk * 128:(kk + 1) * 128],
                            bigT[:, GN0 + kk * 256:GN0 + (kk + 1) * 256],
                            start=(kk == 0), stop=(kk == 31)))
                    t_mm = tok("te")
                    t_ev = emit("sc", [t_mm, *blk_gate] + ([l1tr[0]] if l1tr[0] else []),
                                lambda g: g.copy(stage32[:, 1024:1280], psum[4][:, 0:256]))
                    wo_bank_guard = t_ev
                    for j in range(2):
                        t = ci * 2 + j
                        tt = emit("te", [t_ev, t_id32] + ([l1tr[0]] if l1tr[0] else []),
                                  lambda g, j=j: g.transpose(
                            psum[6][:, 128:256],
                            stage32[:, 1024 + j * 128:1024 + j * 128 + 128], idsb32[:]))
                        l1tr[0] = emit("ve", [tt], lambda g, t=t, d128=d128: g.tensor_tensor(
                            xres(t)[:, d128 * 128:(d128 + 1) * 128],
                            xres(t)[:, d128 * 128:(d128 + 1) * 128],
                            psum[6][:, 128:256], op=ALU.add))
                    wo_strip_guard = tok("te")
                blk_gate = [tok("te"), tok("ve"), tok("sc")]
            attn1_done = [tok("te"), tok("ve"), tok("sc"), P.ring_join("sy")]

        if stage == 4 or stage >= 6:
            gh16 = xnT[:, 0:16384].bitcast(F16)
            ff_e4 = []
            for i in range(NT):
                ec, _ = sm_col()
                ff_e4.append(ec)
            l1ffn_entry = list(attn1_done) if stage >= 5 else [*xq_done]
            tr32f_guard = [l1ffn_entry[1]]
            f1_strip_guard = None
            hq_guard = None
            w2bank_guard = {0: None, 1: None}
            for blk2 in range(2):
                for pair in range(2):
                    i0 = blk2 * 4 + pair * 2
                    gelu_guard = {0: None, 1: None}
                    dxl = list(l1ffn_entry) + ([hq_guard] if hq_guard else [])
                    t_xql = dma("sy", dxl, bigT[:, 32768:32768 + 4096].rearrange(
                        "p (a b) -> p a b", b=256),
                        xq_d.rearrange("(a p) t -> p a t", p=128)[:, :, i0 * 128:i0 * 128 + 256])
                    for fc in range(32):
                        dstr = [tq_f1, t_xql] + ([f1_strip_guard] if f1_strip_guard else [])
                        t_f1s = dma("sy", dstr, stage16[:, 0:4096],
                                    f1q_d.rearrange("(k p) h -> p k h", p=128)[:, :, fc * 256:(fc + 1) * 256])
                        for it in range(2):
                            i = i0 + it
                            bank = psum[it]
                            dmm = [t_f1s, t_xql] + ([gelu_guard[it]] if gelu_guard[it] else [])
                            for kk in range(16):
                                dd = dmm if kk == 0 else []
                                emit("te", dd, lambda g, kk=kk, it=it, fc=fc, bank=bank: g.matmul(
                                    bank[:, 0:256],
                                    bigT[:, 32768 + kk * 256 + it * 128:32768 + kk * 256 + it * 128 + 128],
                                    stage16[:, kk * 256:(kk + 1) * 256],
                                    start=(kk == 0), stop=(kk == 15)))
                            t_hmm = tok("te")
                            gelu_guard[it] = emit("sc", [t_hmm, xq_done[1]],
                                 lambda g, it=it, fc=fc, bank=bank, i=i: g.activation(
                                gh16[:, it * 8192 + fc * 256:it * 8192 + (fc + 1) * 256],
                                bank[:, 0:256], AF.Gelu, scale=ff_e2[i]))
                        f1_strip_guard = tok("te")
                    t_gh = tok("sc")
                    for it in range(2):
                        i = i0 + it
                        tcol = pair * 2 + it
                        t_mx = emit("ve", [t_gh], lambda g, it=it: g.tensor_reduce(
                            sm[:, 24:25], gh16[:, it * 8192:(it + 1) * 8192], axis=AX.X,
                            op=ALU.max, apply_absolute_value=True))
                        t_mx = emit("ve", [t_mx], lambda g: g.tensor_scalar(
                            sm[:, 24:25], sm[:, 24:25], 1e-5, None, op0=ALU.max))
                        t_c3 = emit("ve", [t_mx], lambda g: g.reciprocal(sm[:, 25:26], sm[:, 24:25]))
                        t_c3 = emit("ve", [t_c3], lambda g: g.tensor_scalar(
                            sm[:, 25:26], sm[:, 25:26], 127.0, None, op0=ALU.mult))
                        t_e4 = emit("ve", [t_mx], lambda g, i=i: g.tensor_scalar(
                            ff_e4[i], sm[:, 24:25], float(1.0 / 127.0), None, op0=ALU.mult))
                        tq_prev = tok("te")
                        for c8 in range(16):
                            dq = [t_c3] + ([tq_prev] if tq_prev else [])
                            t_i = emit("ve", dq, lambda g, it=it, c8=c8: g.tensor_scalar(
                                i32st[:, 0:512],
                                gh16[:, it * 8192 + c8 * 512:it * 8192 + (c8 + 1) * 512],
                                sm[:, 25:26], None, op0=ALU.mult))
                            t_b = emit("ve", [t_i, t_e4], lambda g, i=i: g.tensor_scalar(
                                evac16[:, 0:512], i32st[:, 0:512], ff_e4[i], None,
                                op0=ALU.mult))
                            for dd4 in range(4):
                                hten = c8 * 4 + dd4
                                transpose128r(
                                    t_b, evac16[:, dd4 * 128:(dd4 + 1) * 128],
                                    bigT[0:128, hten * 512 + tcol * 128:hten * 512 + tcol * 128 + 128])
                            tq_prev = tok("te")
                    hq_guard = tok("sc")
                hq_done = [tok("te"), tok("sc")]
                f2_strip_guard = None
                for d128 in range(16):
                    for ks in range(2):
                        dstr = [tq_f2, tok("te")] + ([f2_strip_guard] if f2_strip_guard else [])
                        t_f2s = dma("sy", dstr, stage16[:, 0:4096],
                                    f2q_d[ks * 4096:(ks + 1) * 4096, :].rearrange(
                                        "(f p) c -> p f c", p=128)[:, :, d128 * 128:(d128 + 1) * 128])
                        bank = psum[4]
                        first = [t_f2s, *hq_done]
                        if ks == 0 and w2bank_guard[0]:
                            first.append(w2bank_guard[0])
                        for f in range(32):
                            ht = ks * 32 + f
                            d = first if f == 0 else []
                            emit("te", d, lambda g, f=f, ht=ht, bank=bank: g.matmul(
                                bank[:, 0:512],
                                stage16[:, f * 128:(f + 1) * 128],
                                bigT[:, ht * 512:(ht + 1) * 512],
                                start=(ht == 0), stop=(ht == 63)))
                        f2_strip_guard = tok("te")
                    t_mm = tok("te")
                    dev_ = [t_mm, tq_f2] + ([tr32f_guard[0]] if tr32f_guard[0] else [])
                    t_ev = emit("sc", dev_, lambda g: g.activation(
                        stage32[:, 1024:1536], psum[4][:, 0:512], AF.Identity, scale=sf2_col))
                    w2bank_guard[0] = t_ev
                    for j in range(4):
                        t = blk2 * 4 + j
                        dtt = [t_ev, t_id32] + ([tr32f_guard[0]] if tr32f_guard[0] else [])
                        tt = emit("te", dtt, lambda g, j=j: g.transpose(
                            psum[6][:, 128:256],
                            stage32[:, 1024 + j * 128:1024 + j * 128 + 128],
                            idsb32[:]))
                        ta = emit("ve", [tt], lambda g, t=t, d128=d128: g.tensor_tensor(
                            xres(t)[:, d128 * 128:(d128 + 1) * 128],
                            xres(t)[:, d128 * 128:(d128 + 1) * 128],
                            psum[6][:, 128:256], op=ALU.add))
                        tr32f_guard[0] = ta
                f1_strip_guard = tok("te")
            ffn1_done = [tok("ve")]

        st_deps = [tok("ve"), tok("sc"), tok("te")]
        t_sts = []
        for i in range(NT):
            t_sts.append(dma("gp", st_deps, out_own[i * 128:(i + 1) * 128, :], xres(i)))
        tapbufs = {"xnT": xnT, "bigT": bigT, "sm": sm, "stage16": stage16,
                   "stage32": stage32, "evac16": evac16, "x_res": x_res,
                   "w1q": w1q_d, "w2q": w2q_d, "w3q": w3q_d,
                   "f1q": f1q_d, "f2q": f2q_d, "xq": xq_d}
        for tp in taps:
            if tp in ("dbg32", "dbgtri", "dbgw"):
                continue
            src = tapbufs[tp]
            shp = list(src.shape) if hasattr(src, "shape") else None
            dt = src.dtype
            o = nc.dram_tensor("tap_" + tp, shp, dt, kind="ExternalOutput").ap()
            dma("gp", st_deps, o, src[:] if tp not in ("w1q", "w2q", "w3q") else src)
        emit("gp", [tok("gp"), tok("sy")], lambda g: g.nop())
        P.finalize(blk, None)

    return nc



def _prep_consts():
    ident = np.eye(128, dtype=ml_dtypes.bfloat16)
    ident32 = np.eye(128, dtype=np.float32)
    v = np.arange(896)[None, :]
    i = np.arange(128)[:, None]
    strip = (v >= i + 384).astype(ml_dtypes.bfloat16)
    tri = np.zeros((128, 512), np.float32)
    for h in range(2):
        ii = 128 * h + i
        ll = np.arange(256)[None, :]
        tri[:, h * 256:(h + 1) * 256] = np.where(ii <= ll, 0.0, -1e9)
    return ident, ident32, strip, tri.astype(ml_dtypes.bfloat16)


def _in_maps(inputs):
    f32 = np.float32
    asf = lambda a: np.ascontiguousarray(a, dtype=f32)
    x = asf(inputs["L0_attn_x"])
    ident, ident32, strip, tri = _prep_consts()
    shared = dict(
        W_latE=np.concatenate([asf(inputs["L0_attn_W_lat"]),
                               np.zeros((C, 1), np.float32)], axis=1),
        blatE=np.concatenate([asf(inputs["L0_attn_b_lat"]).reshape(1, LAT),
                              np.ones((1, 1), np.float32)], axis=1),
        W_d=asf(inputs["L0_attn_W_d"]),
        b_d=np.ascontiguousarray(asf(inputs["L0_attn_b_d"]).reshape(16, 128).T),
        W_proj=asf(inputs["L0_attn_W_proj"]),
        b_proj=np.ascontiguousarray(asf(inputs["L0_attn_b_proj"]).reshape(16, 128).T),
        w1T=asf(inputs["L0_ffn_w1"].T),
        w3T=asf(inputs["L0_ffn_w3"].T),
        w2T=asf(inputs["L0_ffn_w2"].T),
        W_inT=asf(inputs["L1_attn_W_in"].T),
        Wconv=asf(inputs["L1_attn_W_conv"]).reshape(CONV_DIM, D_CONV),
        b_conv=np.ascontiguousarray(
            asf(inputs["L1_attn_b_conv"]).reshape(CONV_DIM // 128, 128).T),
        dt_bias=asf(inputs["L1_attn_dt_bias"]).reshape(NHEADS, 1),
        A_log=asf(inputs["L1_attn_A_log"]).reshape(NHEADS, 1),
        D_rep=np.ascontiguousarray(np.broadcast_to(
            asf(inputs["L1_attn_D"]).reshape(1, NHEADS), (128, NHEADS))),
        norm_w=np.ascontiguousarray(asf(inputs["L1_attn_norm_w"]).reshape(NHEADS, 128).T),
        W_outT=asf(inputs["L1_attn_W_out"].T),
        f1T=asf(inputs["L1_ffn_w1"].T),
        f2T=asf(inputs["L1_ffn_w2"].T),
        gamma_row=asf(inputs["L1_ffn_gamma"]).reshape(1, C),
        gamma_rep=np.ascontiguousarray(np.broadcast_to(
            asf(inputs["L1_ffn_gamma"]).reshape(1, C), (128, C))),
        ident=ident, ident32=ident32, strip=strip, trineg=tri,
    )
    maps = []
    for core in range(8):
        b, q = divmod(core, 4)
        m = dict(shared)
        m["x_own"] = np.ascontiguousarray(x[b, q * TQ:(q + 1) * TQ, :])
        m["x_full"] = np.ascontiguousarray(x[b])
        m["qcfg"] = np.array([[q, 0, 0, 0]], dtype=np.uint32)
        maps.append(m)
    return maps


def kernel(**inputs):
    nc = build_nc()
    maps = _in_maps(inputs)
    res = run_bass_kernel_spmd(nc, maps, core_ids=list(range(8)))
    out = np.empty((B, T, C), np.float32)
    for core in range(8):
        b, q = divmod(core, 4)
        out[b, q * TQ:(q + 1) * TQ, :] = res.results[core]["out_own"]
    if DEV_TAPS:
        kernel.tap_results = res.results
    return out


kernel.tap_results = None
